# version 02
# baseline (speedup 1.0000x reference)
import math
import contextlib
import numpy as np
import ml_dtypes
import concourse.bass as bass
import concourse.mybir as mybir
from concourse.alu_op_type import AluOpType as OP
from concourse.bass_utils import run_bass_kernel_spmd

F32 = mybir.dt.float32
BF16 = mybir.dt.bfloat16
AF = mybir.ActivationFunctionType
AX = mybir.AxisListType

S = 2048
D = 2048
H0 = 16
DH0 = 128
NSEL = 717
KPAD = 768
HF = 8192
H1 = 16
DH1 = 128
NE = 8
DI = 1408
SC0 = 1.0 / math.sqrt(128.0)
SC1 = 64.0 ** -0.5
LAMBDA_INIT = float(0.8 - 0.6 * math.exp(-3.6))
RMS_EPS = float(np.finfo(np.float32).eps)
MAGIC = 12582912.0
NEG = -1.0e30
BF16_1EM8 = float(np.float32(np.asarray(1e-8, ml_dtypes.bfloat16)))

ENGS = ("tensor", "vector", "scalar", "gpsimd", "sync")

KSEL_TILES = [(0, 128, 0), (128, 77, 128), (1536, 128, 205), (1664, 128, 333),
              (1792, 128, 461), (1920, 128, 589)]

PHASES_ALL = ("conv", "quant", "ln0", "l0qkv", "l0attn", "l0wo", "l0ffn",
              "ln1", "l1qkv", "l1attn", "l1wo", "l1moe")

SCRATCH = {
    "xn_tok":  ([S, D], BF16),
    "q0_tok":  ([S, D], BF16),
    "k0_tok":  ([KPAD, D], BF16),
    "v0_tok":  ([KPAD, D], BF16),
    "oT0":     ([D, S], BF16),
    "s0":      ([S, D], F32),
    "x1":      ([S, D], F32),
    "xn1_tok": ([S, D], BF16),
    "q1_tok":  ([S, D], BF16),
    "k1_tok":  ([S, D], BF16),
    "v1_tok":  ([S, D], BF16),
    "logits":  ([S, 8], F32),
    "oT1":     ([D, S], BF16),
    "s1":      ([S, D], F32),
    "wq_bf":   ([D, D], BF16),
    "wk_bf":   ([D, D], BF16),
    "wv_bf":   ([D, D], BF16),
    "wo_bf":   ([D, D], BF16),
    "wgu_hat": ([2 * HF, D], BF16),
    "wp_hat":  ([D, HF], BF16),
    "wqkv_bf": ([3 * D, D], BF16),
    "wo1_bf":  ([D, D], BF16),
    "sw1_bf":  ([DI, D], BF16),
    "sw2_bf":  ([DI, D], BF16),
    "sw3_bf":  ([D, DI], BF16),
    "w1_bf":   ([NE * D, 2 * DI], BF16),
    "w2_bf":   ([NE * DI, D], BF16),
    "gate_pad": ([16, D], BF16),
    "dbg_cw":  ([S, 8], F32),
}

MODINS = {
    "x":      ([S, D], F32),
    "wq":     ([D, D], F32),
    "wk":     ([D, D], F32),
    "wv":     ([D, D], F32),
    "wo":     ([D, D], F32),
    "bq":     ([1, D], F32),
    "bk":     ([1, D], F32),
    "bv":     ([1, D], F32),
    "bo":     ([1, D], F32),
    "wgu":    ([2 * HF, D], F32),
    "wp":     ([D, HF], F32),
    "wqkv":   ([3 * D, D], F32),
    "wo1":    ([D, D], F32),
    "lq1":    ([1, 128], F32),
    "lk1":    ([1, 128], F32),
    "lq2":    ([1, 128], F32),
    "lk2":    ([1, 128], F32),
    "gng":    ([16, 128], F32),
    "gnb":    ([16, 128], F32),
    "gatew":  ([8, D], F32),
    "w1":     ([NE * D, 2 * DI], F32),
    "w2":     ([NE * DI, D], F32),
    "sw1":    ([DI, D], F32),
    "sw2":    ([DI, D], F32),
    "sw3":    ([D, DI], F32),
    "c_ones": ([1, 512], F32),
    "c_idf":  ([128, 128], F32),
    "c_idb":  ([128, 128], BF16),
    "c_cmask": ([128, 128], F32),
    "c_scq":  ([S, 512], F32),
}


class Sem:
    __slots__ = ("name", "n", "h")

    def __init__(self, name):
        self.name, self.n, self.h = name, 0, None


CHAIN_ENGS = ("vector", "scalar", "gpsimd")


class Ctx:

    def __init__(self, nc):
        self.nc = nc
        self.ops = {e: [] for e in ENGS}
        self.sems = []
        self.t = {}
        self.sb = {}
        self.ps = {}
        self.auto = {e: self.sem("AUTO_" + e[0]) for e in CHAIN_ENGS}
        self.last = {}
        self.lu = {}

    def sem(self, name):
        s = Sem(name)
        self.sems.append(s)
        return s

    def emit(self, eng, fn):
        self.ops[eng].append(fn)

    def op(self, eng, meth, *args, sem=None, inc=1, chain=True, **kw):
        is_dma = meth.startswith("dma_start") or meth == "indirect_dma_start"
        chaining = eng in CHAIN_ENGS and chain and not is_dma
        if chaining:
            prev = self.last.get(eng)
            if prev is not None:
                self.wait(eng, prev[0], prev[1])
            if sem is None:
                sem, inc = self.auto[eng], 1
        if sem is not None:
            sem.n += inc

        def f(e, meth=meth, args=args, kw=kw, s=sem, inc=inc):
            i = getattr(e, meth)(*args, **kw)
            if s is not None:
                i.then_inc(s.h, inc)
        self.emit(eng, f)
        if chaining:
            self.last[eng] = (sem, sem.n)

    def wait(self, eng, sem, at=None):
        v = sem.n if at is None else at
        if v is None or v <= 0:
            return
        self.emit(eng, lambda e, s=sem, v=v: e.wait_ge(s.h, v))

    def dma(self, q, dst, src, sem, **kw):
        self.op(q, "dma_start", dst, src, sem=sem, inc=16, **kw)

    def mm(self, out, lhsT, rhs, start, stop, sem=None, **kw):
        kw.setdefault("skip_group_check", True)
        self.op("tensor", "matmul", out, lhsT, rhs, sem=sem,
                start=start, stop=stop, **kw)

    def fence(self):
        for eng in ENGS:
            for s in self.sems:
                self.wait(eng, s)


def prep_inputs(inputs):
    g = {}
    g["wq"] = inputs["L0_attn_Wq"]
    g["wk"] = inputs["L0_attn_Wk"]
    g["wv"] = inputs["L0_attn_Wv"]
    g["wo"] = inputs["L0_attn_Wo"]
    g["bq"] = inputs["L0_attn_bq"].reshape(1, D)
    g["bk"] = inputs["L0_attn_bk"].reshape(1, D)
    g["bv"] = inputs["L0_attn_bv"].reshape(1, D)
    g["bo"] = inputs["L0_attn_bo"].reshape(1, D)
    g["wgu"] = inputs["L0_ffn_w_gate_up"]
    g["wp"] = inputs["L0_ffn_w_proj"]
    g["wqkv"] = inputs["L1_attn_W_qkv"]
    g["wo1"] = inputs["L1_attn_W_o"]
    g["lq1"] = inputs["L1_attn_lambda_q1"].reshape(1, 128)
    g["lk1"] = inputs["L1_attn_lambda_k1"].reshape(1, 128)
    g["lq2"] = inputs["L1_attn_lambda_q2"].reshape(1, 128)
    g["lk2"] = inputs["L1_attn_lambda_k2"].reshape(1, 128)
    g["gng"] = inputs["L1_attn_gn_gamma"]
    g["gnb"] = inputs["L1_attn_gn_beta"]
    g["gatew"] = inputs["L1_ffn_gate_w"]
    g["w1"] = np.ascontiguousarray(inputs["L1_ffn_w1"].reshape(NE * D, 2 * DI))
    g["w2"] = np.ascontiguousarray(inputs["L1_ffn_w2"].reshape(NE * DI, D))
    g["sw1"] = inputs["L1_ffn_sw1"]
    g["sw2"] = inputs["L1_ffn_sw2"]
    g["sw3"] = inputs["L1_ffn_sw3"]
    g["c_ones"] = np.ones((1, 512), np.float32)
    g["c_idf"] = np.eye(128, dtype=np.float32)
    g["c_idb"] = np.eye(128).astype(ml_dtypes.bfloat16)
    cm = np.zeros((128, 128), np.float32)
    cm[np.triu_indices(128, 1)] = NEG
    g["c_cmask"] = cm
    theta = 1.0 / 10000.0 ** (np.arange(0, 128, 2, dtype=np.float32) / 128.0)
    sinu = np.arange(S, dtype=np.float32)[:, None] * theta[None, :]
    sin = np.sin(sinu).astype(np.float32)
    cos = np.cos(sinu).astype(np.float32)
    g["c_scq"] = np.concatenate([np.tile(sin, (1, 4)), np.tile(cos, (1, 4))],
                                axis=1)
    for k in g:
        want = MODINS[k]
        assert list(g[k].shape) == want[0], (k, g[k].shape, want)
        g[k] = np.ascontiguousarray(g[k])
    return g


def declare_tensors(c, phases, feed, want):
    nc = c.nc
    for name, (shape, dt) in MODINS.items():
        c.t[name] = nc.dram_tensor(name, shape, dt, kind="ExternalInput").ap()
    for name, (shape, dt) in SCRATCH.items():
        if name in feed:
            kind = "ExternalInput"
        elif name in want:
            kind = "ExternalOutput"
        else:
            kind = "Internal"
        c.t[name] = nc.dram_tensor(name, shape, dt, kind=kind).ap()
    c.t["out"] = nc.dram_tensor("out", [S, D], F32, kind="ExternalOutput").ap()


def alloc_onchip(c, stk):
    nc = c.nc
    E = stk.enter_context

    def sb(name, shape, dt):
        c.sb[name] = E(nc.sbuf_tensor(name, shape, dt))

    def ps(name, shape, dt):
        c.ps[name] = E(nc.psum_tensor(name, shape, dt))

    sb("XT", [128, 8192], BF16)
    sb("BIG", [128, 32768], BF16)
    sb("MED", [128, 16384], BF16)
    sb("WA", [128, 8192], BF16)
    sb("WB", [128, 8192], BF16)
    sb("EA", [128, 2048], F32)
    sb("EB", [128, 2048], F32)
    sb("QSB", [128, 512], BF16)
    sb("SGB", [128, 512], BF16)
    sb("ST1", [128, 8], F32)
    sb("ST2", [128, 8], F32)
    sb("RR", [128, 4], F32)
    sb("SCQ", [128, 2048], F32)
    sb("IDF", [128, 128], F32)
    sb("IDB", [128, 128], BF16)
    sb("CMASK", [128, 128], F32)
    sb("CWT", [128, 128], F32)
    sb("LT", [128, 16], F32)
    sb("GW", [8, 2048], BF16)
    sb("GWT", [128, 256], BF16)
    sb("ROWA", [1, 2048], F32)
    sb("ROWB", [1, 2048], F32)
    sb("ROWC", [1, 512], F32)
    sb("ROWD", [1, 128], F32)
    sb("ROWE", [1, 128], F32)
    sb("NLB", [128, 128], BF16)
    sb("GB", [128, 128], F32)
    sb("BB", [128, 128], F32)
    sb("SCL", [128, 128], BF16)
    sb("RSC", [128, 256], F32)
    sb("CONSTF", [128, 8], F32)

    ps("PSB", [128, 2048], F32)
    ps("PF0", [128, 512], F32)
    ps("PF1", [128, 512], F32)
    ps("PF2", [128, 512], F32)
    ps("PF3", [128, 512], F32)



def ph_conv(c, sm):
    g = sm["GLD"]
    pairs = [("wq", "wq_bf"), ("wk", "wk_bf"), ("wv", "wv_bf"),
             ("wo", "wo_bf"), ("wqkv", "wqkv_bf"), ("wo1", "wo1_bf"),
             ("sw1", "sw1_bf"), ("sw2", "sw2_bf"), ("sw3", "sw3_bf")]
    for a, b in pairs:
        c.dma("gpsimd", c.t[b], c.t[a], sem=g)
    for i in range(8):
        c.dma("gpsimd", c.t["w1_bf"][i * D:(i + 1) * D, :],
              c.t["w1"][i * D:(i + 1) * D, :], sem=g)
        c.dma("gpsimd", c.t["w2_bf"][i * DI:(i + 1) * DI, :],
              c.t["w2"][i * DI:(i + 1) * DI, :], sem=g)


def _quant_one(c, sm, src, dst, nrows, W):
    G = W // 64
    BIG, MED = c.sb["BIG"], c.sb["MED"]
    SCL, RSC = c.sb["SCL"], c.sb["RSC"]
    GLD, VE, SC, ST = sm["GLD"], sm["VE"], sm["SC"], sm["ST"]
    MEDf = MED[:, :].bitcast(F32)
    for r0 in range(0, nrows, 128):
        qw = BIG[:, 0:W]
        qb = BIG[:, 8192:8192 + W]
        dd = BIG[:, 16384:16384 + W]
        med = MEDf[:, 0:W]
        c.dma("gpsimd", qw, src[r0:r0 + 128, :], sem=GLD)
        c.wait("scalar", GLD)
        c.op("scalar", "activation", med, qw, AF.Abs, sem=SC)
        c.wait("vector", SC)
        m3 = med.rearrange("p (g s) -> p g s", s=64)
        c.op("vector", "reduce_sum", RSC[:, 0:G], m3, axis=AX.X)
        c.op("vector", "tensor_scalar", SCL[:, 0:G], RSC[:, 0:G],
             1.0 / 64.0, BF16_1EM8, op0=OP.mult, op1=OP.max, sem=VE)
        c.wait("scalar", VE)
        c.op("scalar", "activation", RSC[:, 128:128 + G], SCL[:, 0:G], AF.Ln,
             sem=SC)
        c.wait("scalar", SC)
        c.op("scalar", "activation", RSC[:, 0:G], RSC[:, 128:128 + G], AF.Exp,
             scale=-1.0, sem=SC)
        c.wait("vector", SC)
        w3 = qw.rearrange("p (g s) -> p g s", s=64)
        q3 = qb.rearrange("p (g s) -> p g s", s=64)
        d3 = dd.rearrange("p (g s) -> p g s", s=64)
        med3 = med.rearrange("p (g s) -> p g s", s=64)
        rsb = RSC[:, 0:G].unsqueeze(2).broadcast_to([128, G, 64])
        scb = SCL[:, 0:G].unsqueeze(2).broadcast_to([128, G, 64])
        c.op("vector", "tensor_tensor", med3, w3, rsb, op=OP.mult)
        c.op("vector", "tensor_copy", qb, med)
        c.op("vector", "tensor_scalar", med, qb, MAGIC, MAGIC,
             op0=OP.add, op1=OP.subtract)
        c.op("vector", "tensor_scalar", med, med, 1.0, -1.0,
             op0=OP.min, op1=OP.max)
        c.op("vector", "tensor_tensor", q3, med3, scb, op=OP.mult)
        c.op("vector", "tensor_tensor", dd, qb, qw, op=OP.subtract)
        c.op("vector", "tensor_tensor", qb, qw, dd, op=OP.add, sem=VE)
        c.wait("sync", VE)
        c.dma("sync", dst[r0:r0 + 128, :], qb, sem=ST)
        c.wait("gpsimd", ST)
        c.wait("scalar", ST)


def ph_quant(c, sm):
    _quant_one(c, sm, c.t["wgu"], c.t["wgu_hat"], 2 * HF, D)
    _quant_one(c, sm, c.t["wp"], c.t["wp_hat"], D, HF)


def ph_ln(c, sm, src_name, dst_name):
    EA, EB, BIG = c.sb["EA"], c.sb["EB"], c.sb["BIG"]
    ST1 = c.sb["ST1"]
    LD, VE, SC, ST = sm["LD"], sm["VE"], sm["SC"], sm["ST"]
    src, dst = c.t[src_name], c.t[dst_name]
    MEDf = c.sb["MED"][:, :].bitcast(F32)
    ve_after_reduce = 0
    for tt in range(16):
        xa = EA[:, :]
        c.wait("sync", VE)
        c.wait("sync", SC)
        c.dma("sync", xa, src[tt * 128:(tt + 1) * 128, :], sem=LD)
        c.wait("vector", LD)
        c.wait("scalar", LD)
        c.op("vector", "reduce_sum", ST1[:, 0:1], xa, axis=AX.X)
        c.wait("scalar", VE, ve_after_reduce)
        c.op("scalar", "activation", MEDf[:, 0:2048], xa, AF.Square, sem=SC)
        c.wait("vector", SC)
        c.op("vector", "reduce_sum", ST1[:, 1:2], MEDf[:, 0:2048], axis=AX.X,
             sem=VE)
        ve_after_reduce = VE.n
        c.op("vector", "tensor_scalar", ST1[:, 2:3], ST1[:, 0:1], 1.0 / D, None,
             op0=OP.mult)
        c.op("vector", "tensor_scalar", ST1[:, 3:4], ST1[:, 2:3], ST1[:, 2:3],
             None, op0=OP.mult)
        c.op("vector", "tensor_scalar", ST1[:, 4:5], ST1[:, 1:2], 1.0 / D,
             ST1[:, 3:4], op0=OP.mult, op1=OP.subtract, sem=VE)
        c.wait("scalar", VE)
        c.op("scalar", "activation", ST1[:, 5:6], ST1[:, 4:5], AF.Ln,
             bias=1e-5, sem=SC)
        c.wait("scalar", SC)
        c.op("scalar", "activation", ST1[:, 6:7], ST1[:, 5:6], AF.Exp,
             scale=-0.5, sem=SC)
        c.wait("vector", SC)
        xnb = BIG[:, (tt % 2) * 2048:(tt % 2) * 2048 + 2048]
        c.wait("vector", ST, max(0, sm["ST"].n - 16))
        c.op("vector", "tensor_scalar", xnb, xa, ST1[:, 2:3], ST1[:, 6:7],
             op0=OP.subtract, op1=OP.mult, sem=VE)
        c.wait("sync", VE)
        c.dma("sync", dst[tt * 128:(tt + 1) * 128, :], xnb, sem=ST)


def _load_xnT(c, sm, src, tg):
    XT = c.sb["XT"]
    LD = sm["LD"]
    for dc in range(16):
        c.dma("sync", XT[:, dc * 512:(dc + 1) * 512],
              src[tg * 512:(tg + 1) * 512, dc * 128:(dc + 1) * 128],
              sem=LD, transpose=True)


def ph_l0qkv(c, sm):
    XT, WA, BIG, EA, QSB, SCQ = (c.sb["XT"], c.sb["WA"], c.sb["BIG"],
                                 c.sb["EA"], c.sb["QSB"], c.sb["SCQ"])
    ROWA, ROWC = c.sb["ROWA"], c.sb["ROWC"]
    LD, MM, VE, SC, ST = sm["LD"], sm["MM"], sm["VE"], sm["SC"], sm["ST"]
    PF = [c.ps["PF0"], c.ps["PF1"], c.ps["PF2"], c.ps["PF3"]]
    xn = c.t["xn_tok"]
    c.op("vector", "memset", BIG[:, 0:2048], 0, sem=VE)
    c.wait("sync", VE)
    c.dma("sync", c.t["k0_tok"][KPAD - 51:KPAD, :], BIG[0:51, 0:2048], sem=ST)
    c.dma("sync", c.t["v0_tok"][KPAD - 51:KPAD, :], BIG[0:51, 0:2048], sem=ST)
    c.dma("sync", ROWC[:, :], c.t["c_ones"], sem=LD)

    kv_in_tg = {0: [0, 1], 3: [0, 1, 2, 3]}

    for tg in range(4):
        c.wait("sync", MM)
        _load_xnT(c, sm, xn, tg)
        c.dma("sync", SCQ[:, :].rearrange("p (q x) -> p q x", q=4),
              c.t["c_scq"][tg * 512:(tg + 1) * 512, :]
              .rearrange("(q p) x -> p q x", p=128), sem=LD)
        parts = [("wq_bf", "bq", "q"), ("wk_bf", "bk", "k"), ("wv_bf", "bv", "v")]
        for wname, bname, which in parts:
            if which != "q" and tg not in kv_in_tg:
                continue
            c.wait("sync", MM)
            c.dma("sync", ROWA[:, :], c.t[bname], sem=LD)
            for es in range(4):
                wbuf = c.sb["WA"] if es % 2 == 0 else c.sb["WB"]
                c.wait("sync", MM, c.lu.get(("qkvw", wbuf is c.sb["WB"])))
                for dc in range(16):
                    c.dma("sync", wbuf[:, dc * 512:(dc + 1) * 512],
                          c.t[wname][es * 512:(es + 1) * 512,
                                     dc * 128:(dc + 1) * 128],
                          sem=LD, transpose=True)
                ld_here = LD.n
                qts = range(4) if which == "q" else kv_in_tg[tg]
                for qt in qts:
                    pf = PF[qt % 4]
                    c.wait("tensor", LD, ld_here)
                    c.wait("tensor", SC)
                    c.wait("tensor", VE)
                    for dc in range(16):
                        c.mm(pf[:, :], XT[:, dc * 512 + qt * 128:dc * 512 + qt * 128 + 128],
                             wbuf[:, dc * 512:(dc + 1) * 512],
                             start=(dc == 0), stop=False)
                    c.mm(pf[:, :], ROWC[0:1, 0:128],
                         ROWA[0:1, es * 512:(es + 1) * 512],
                         start=False, stop=True, sem=MM, skip_group_check=True)
                    c.lu[("qkvw", wbuf is c.sb["WB"])] = MM.n
                    tok0 = tg * 512 + qt * 128
                    if which == "v":
                        c.wait("scalar", MM)
                        c.op("scalar", "activation", QSB[:, :], pf[:, :],
                             AF.Copy, sem=SC)
                        c.wait("sync", SC)
                        ti = [i for i, (t0, n, r0) in enumerate(KSEL_TILES)
                              if t0 == tok0][0]
                        t0_, n_, r0_ = KSEL_TILES[ti]
                        c.dma("sync", c.t["v0_tok"][r0_:r0_ + n_,
                                                    es * 512:(es + 1) * 512],
                              QSB[0:n_, :], sem=ST)
                    else:
                        c.wait("vector", MM)
                        sc_s = SCQ[:, qt * 512:qt * 512 + 256] \
                            .rearrange("p (h d) -> p h d", h=4)
                        sc_c = SCQ[:, qt * 512 + 256:qt * 512 + 512] \
                            .rearrange("p (h d) -> p h d", h=4)
                        pev = pf[:, :].rearrange("p (h d j) -> p h d j", d=64, j=2)[:, :, :, 0]
                        pod = pf[:, :].rearrange("p (h d j) -> p h d j", d=64, j=2)[:, :, :, 1]
                        t1 = EA[:, 0:256].rearrange("p (h d) -> p h d", h=4)
                        t2 = EA[:, 256:512].rearrange("p (h d) -> p h d", h=4)
                        t3 = EA[:, 512:768].rearrange("p (h d) -> p h d", h=4)
                        t4 = EA[:, 768:1024].rearrange("p (h d) -> p h d", h=4)
                        ob = QSB[:, :].rearrange("p (h d) -> p h d", h=4, d=128)
                        o1 = ob[:, :, 0:64]
                        o2 = ob[:, :, 64:128]
                        c.op("vector", "tensor_tensor", t1, pev, sc_c, op=OP.mult)
                        c.op("vector", "tensor_tensor", t2, pod, sc_s, op=OP.mult)
                        c.op("vector", "tensor_tensor", t3, pev, sc_s, op=OP.mult)
                        c.op("vector", "tensor_tensor", t4, pod, sc_c, op=OP.mult)
                        c.op("vector", "tensor_tensor", o1, t1, t2, op=OP.subtract)
                        c.op("vector", "tensor_tensor", o2, t3, t4, op=OP.add, sem=VE)
                        c.wait("sync", VE)
                        if which == "q":
                            c.dma("sync", c.t["q0_tok"][tok0:tok0 + 128,
                                                        es * 512:(es + 1) * 512],
                                  QSB[:, :], sem=ST)
                        else:
                            ti = [i for i, (t0, n, r0) in enumerate(KSEL_TILES)
                                  if t0 == tok0][0]
                            t0_, n_, r0_ = KSEL_TILES[ti]
                            c.dma("sync", c.t["k0_tok"][r0_:r0_ + n_,
                                                        es * 512:(es + 1) * 512],
                                  QSB[0:n_, :], sem=ST)
                    c.wait("vector", ST)
                    c.wait("scalar", ST)


def ph_l0attn(c, sm):
    BIG, MED, EA = c.sb["BIG"], c.sb["MED"], c.sb["EA"]
    ST1 = c.sb["ST1"]
    IDB = c.sb["IDB"]
    LD, MM, VE, SC, ST = sm["LD"], sm["MM"], sm["VE"], sm["SC"], sm["ST"]
    PSB, PF0, PF1 = c.ps["PSB"], c.ps["PF0"], c.ps["PF1"]
    kT = BIG[:, 0:768]
    vh = BIG[:, 1024:1024 + 768]
    qT = BIG[:, 2048:4096]
    abf = BIG[:, 4096:4864]
    atb = BIG[:, 4864:5632]
    c.dma("sync", IDB[:, :], c.t["c_idb"], sem=LD)
    for h in range(16):
        c.wait("sync", MM)
        c.dma("sync", kT, c.t["k0_tok"][0:768, h * 128:(h + 1) * 128],
              sem=LD, transpose=True)
        c.dma("sync", vh.rearrange("p (k d) -> p k d", k=6),
              c.t["v0_tok"][:, h * 128:(h + 1) * 128]
              .rearrange("(k p) d -> p k d", p=128), sem=LD)
        c.dma("sync", qT, c.t["q0_tok"][:, h * 128:(h + 1) * 128],
              sem=LD, transpose=True)
        for qt in range(16):
            c.wait("tensor", LD)
            c.wait("tensor", SC)
            c.mm(PSB[:, 0:512], qT[:, qt * 128:(qt + 1) * 128], kT[:, 0:512],
                 start=True, stop=True)
            c.mm(PSB[:, 512:768], qT[:, qt * 128:(qt + 1) * 128], kT[:, 512:768],
                 start=True, stop=True, sem=MM)
            c.wait("vector", MM)
            c.op("vector", "memset", PSB[:, 717:768], NEG)
            c.op("vector", "reduce_max", ST1[:, 0:1], PSB[:, 0:768], axis=AX.X,
                 negate=True)
            c.op("vector", "tensor_scalar", ST1[:, 1:2], ST1[:, 0:1], SC0, None,
                 op0=OP.mult, sem=VE)
            c.wait("scalar", VE)
            c.wait("scalar", MM)
            c.op("scalar", "activation", abf, PSB[:, 0:768], AF.Exp,
                 bias=ST1[:, 1:2], scale=SC0, accum_out=ST1[:, 2:3])
            c.op("scalar", "activation", ST1[:, 4:5], ST1[:, 2:3], AF.Ln,
                 sem=SC)
            c.wait("scalar", SC)
            c.op("scalar", "activation", ST1[:, 3:4], ST1[:, 4:5], AF.Exp,
                 scale=-1.0, sem=SC)
            c.wait("vector", SC)
            c.op("vector", "tensor_scalar", abf, abf, ST1[:, 3:4], None,
                 op0=OP.mult, sem=VE)
            c.wait("tensor", VE)
            ptb = PF0[:, :].bitcast(BF16)
            for k in range(6):
                c.op("tensor", "transpose", ptb[:, k * 128:(k + 1) * 128],
                     abf[:, k * 128:(k + 1) * 128], IDB[:, :],
                     sem=(MM if k == 5 else None))
            c.wait("vector", MM)
            c.op("vector", "tensor_copy", atb[:, 0:768], ptb[:, 0:768], sem=VE)
            c.wait("tensor", VE)
            for k in range(6):
                c.mm(PF1[:, 0:128], vh[:, k * 128:(k + 1) * 128],
                     atb[:, k * 128:(k + 1) * 128],
                     start=(k == 0), stop=(k == 5), sem=(MM if k == 5 else None))
            c.wait("scalar", MM)
            c.op("scalar", "activation", MED[:, qt * 128:(qt + 1) * 128],
                 PF1[:, 0:128], AF.Copy, sem=SC)
        c.wait("sync", SC)
        c.dma("sync", c.t["oT0"][h * 128:(h + 1) * 128, :], MED[:, 0:2048],
              sem=ST)
        c.wait("scalar", ST, sm["ST"].n)


def ph_wo(c, sm, oT_name, w_name, bias_name, dst_name):
    WA, BIG, EA = c.sb["WA"], c.sb["BIG"], c.sb["EA"]
    ROWA, ROWC = c.sb["ROWA"], c.sb["ROWC"]
    LD, MM, VE, ST = sm["LD"], sm["MM"], sm["VE"], sm["ST"]
    PF = [c.ps["PF0"], c.ps["PF1"], c.ps["PF2"], c.ps["PF3"]]
    oT, wbf, dst = c.t[oT_name], c.t[w_name], c.t[dst_name]
    if bias_name:
        c.dma("sync", ROWA[:, :], c.t[bias_name], sem=LD)
    c.dma("sync", ROWC[:, :], c.t["c_ones"], sem=LD)
    wo_bufs = (WA, c.sb["WB"])
    for ds in range(4):
        wbuf = wo_bufs[ds % 2]
        c.wait("sync", MM, c.lu.get(("wow", ds % 2)))
        for dc in range(16):
            c.dma("sync", wbuf[:, dc * 512:(dc + 1) * 512],
                  wbf[ds * 512:(ds + 1) * 512, dc * 128:(dc + 1) * 128],
                  sem=LD, transpose=True)
        ld_w = LD.n
        for qt in range(16):
            otb = BIG[:, (qt % 4) * 2048:(qt % 4) * 2048 + 2048]
            c.wait("sync", MM, c.lu.get(("wot", qt % 4)))
            c.dma("sync", otb.rearrange("p (k t) -> p k t", k=16),
                  oT.rearrange("(k p) t -> p k t", p=128)
                  [:, :, qt * 128:(qt + 1) * 128], sem=LD)
            ld_here = LD.n
            pf = PF[qt % 4]
            c.wait("tensor", LD, ld_here)
            c.wait("tensor", VE, c.lu.get(("wop", qt % 4)))
            for dc in range(16):
                c.mm(pf[:, :], otb[:, dc * 128:(dc + 1) * 128],
                     wbuf[:, dc * 512:(dc + 1) * 512],
                     start=(dc == 0), stop=(not bias_name and dc == 15),
                     sem=(MM if (not bias_name and dc == 15) else None))
            if bias_name:
                c.mm(pf[:, :], ROWC[0:1, 0:128],
                     ROWA[0:1, ds * 512:(ds + 1) * 512],
                     start=False, stop=True, sem=MM, skip_group_check=True)
            c.lu[("wot", qt % 4)] = MM.n
            c.lu[("wow", ds % 2)] = MM.n
            c.wait("vector", MM)
            c.wait("vector", ST, max(0, sm["ST"].n - 3 * 16))
            ebuf = EA[:, (qt % 4) * 512:(qt % 4) * 512 + 512]
            c.op("vector", "tensor_copy", ebuf, pf[:, :], sem=VE)
            c.lu[("wop", qt % 4)] = VE.n
            c.wait("sync", VE)
            c.dma("sync", dst[qt * 128:(qt + 1) * 128, ds * 512:(ds + 1) * 512],
                  ebuf, sem=ST)


def ph_l0ffn(c, sm):
    XT, WA, WB, BIG, EA, SGB = (c.sb["XT"], c.sb["WA"], c.sb["WB"],
                                c.sb["BIG"], c.sb["EA"], c.sb["SGB"])
    IDF, RR, ST1 = c.sb["IDF"], c.sb["RR"], c.sb["ST1"]
    LD, MM, VE, SC, ST = sm["LD"], sm["MM"], sm["VE"], sm["SC"], sm["ST"]
    PSB = c.ps["PSB"]
    PF0, PF1, PF2 = c.ps["PF0"], c.ps["PF1"], c.ps["PF2"]
    wgu, wp = c.t["wgu_hat"], c.t["wp_hat"]
    hT = BIG[:, 0:32768]
    c.dma("sync", IDF[:, :], c.t["c_idf"], sem=LD)
    for sp in range(4):
        c.wait("sync", MM)
        _load_xnT(c, sm, c.t["xn_tok"], sp)
        MEDb = c.sb["MED"]
        c.lu[("ffng", 0)] = c.lu[("ffng", 1)] = MM.n
        for hg in range(16):
            gbuf = WA if hg % 2 == 0 else WB
            ubuf = MEDb[:, 0:8192] if hg % 2 == 0 else MEDb[:, 8192:16384]
            c.wait("sync", MM, c.lu.get(("ffng", hg % 2)))
            for dc in range(16):
                c.dma("sync", gbuf[:, dc * 512:(dc + 1) * 512],
                      wgu[hg * 512:(hg + 1) * 512, dc * 128:(dc + 1) * 128],
                      sem=LD, transpose=True)
                c.dma("sync", ubuf[:, dc * 512:(dc + 1) * 512],
                      wgu[HF + hg * 512:HF + (hg + 1) * 512,
                          dc * 128:(dc + 1) * 128],
                      sem=LD, transpose=True)
            ld_here = LD.n
            c.wait("tensor", LD, ld_here)
            for j in range(4):
                hb = hg * 4 + j
                c.wait("tensor", SC)
                for dc in range(16):
                    c.mm(PF0[:, :], gbuf[:, dc * 512 + j * 128:dc * 512 + j * 128 + 128],
                         XT[:, dc * 512:(dc + 1) * 512],
                         start=(dc == 0), stop=(dc == 15),
                         sem=(MM if dc == 15 else None))
                c.wait("tensor", VE)
                for dc in range(16):
                    c.mm(PF1[:, :], ubuf[:, dc * 512 + j * 128:dc * 512 + j * 128 + 128],
                         XT[:, dc * 512:(dc + 1) * 512],
                         start=(dc == 0), stop=(dc == 15),
                         sem=(MM if dc == 15 else None))
                if j == 3:
                    c.lu[("ffng", hg % 2)] = MM.n
                c.wait("scalar", MM, MM.n - 1)
                c.op("scalar", "activation", SGB[:, :], PF0[:, :], AF.Silu,
                     sem=SC)
                c.wait("vector", SC)
                c.wait("vector", MM)
                c.op("vector", "tensor_tensor", hT[:, hb * 512:(hb + 1) * 512],
                     SGB[:, :], PF1[:, :], op=OP.mult, sem=VE)
                c.wait("tensor", VE)
                for st in range(4):
                    c.mm(PF2[:, st * 128:(st + 1) * 128],
                         hT[:, hb * 512 + st * 128:hb * 512 + st * 128 + 128],
                         hT[:, hb * 512 + st * 128:hb * 512 + st * 128 + 128],
                         start=(hb == 0), stop=(hb == 63),
                         sem=(MM if hb == 63 and st == 3 else None))
        c.wait("vector", MM)
        for st in range(4):
            c.op("vector", "tensor_tensor", EA[:, 0:128],
                 PF2[:, st * 128:(st + 1) * 128], IDF[:, :], op=OP.mult)
            c.op("vector", "reduce_sum", ST1[:, 0:1], EA[:, 0:128], axis=AX.X)
            c.op("vector", "tensor_scalar", ST1[:, 1:2], ST1[:, 0:1],
                 1.0 / HF, RMS_EPS, op0=OP.mult, op1=OP.add, sem=VE)
            c.wait("scalar", VE)
            c.op("scalar", "activation", ST1[:, 2:3], ST1[:, 1:2], AF.Ln,
                 sem=SC)
            c.wait("scalar", SC)
            c.op("scalar", "activation", RR[:, st:st + 1], ST1[:, 2:3], AF.Exp,
                 scale=-0.5, sem=SC)
            c.wait("vector", SC)
        gu_done = MM.n
        for ds in range(4):
            for hb in range(64):
                slot = hb % 32
                sbuf_ = WA if slot < 16 else WB
                wslot = sbuf_[:, (slot % 16) * 512:(slot % 16) * 512 + 512]
                c.wait("sync", MM, max(c.lu.get(("ffny", slot), 0), gu_done))
                c.dma("sync", wslot,
                      wp[ds * 512:(ds + 1) * 512, hb * 128:(hb + 1) * 128],
                      sem=LD, transpose=True)
                ld_here = LD.n
                c.wait("tensor", LD, ld_here)
                if hb == 0:
                    c.wait("tensor", SC)
                for st in range(4):
                    c.mm(PSB[:, st * 512:(st + 1) * 512],
                         hT[:, hb * 512 + st * 128:hb * 512 + st * 128 + 128],
                         wslot, start=(hb == 0), stop=(hb == 63),
                         sem=(MM if st == 3 else None))
                c.lu[("ffny", slot)] = MM.n
            c.wait("scalar", MM)
            for st in range(4):
                tok0 = sp * 512 + st * 128
                c.wait("scalar", VE)
                c.op("scalar", "activation", EA[:, 0:512],
                     PSB[:, st * 512:(st + 1) * 512], AF.Silu,
                     scale=RR[:, st:st + 1], sem=SC)
                c.wait("gpsimd", VE)
                c.dma("gpsimd", EA[:, 512:1024],
                      c.t["x"][tok0:tok0 + 128, ds * 512:(ds + 1) * 512],
                      sem=sm["GLD"])
                c.dma("gpsimd", EA[:, 1024:1536],
                      c.t["s0"][tok0:tok0 + 128, ds * 512:(ds + 1) * 512],
                      sem=sm["GLD"])
                c.wait("vector", SC)
                c.wait("vector", sm["GLD"])
                c.wait("vector", ST)
                c.op("vector", "tensor_tensor", EA[:, 1536:2048], EA[:, 0:512],
                     EA[:, 1024:1536], op=OP.mult)
                c.op("vector", "tensor_tensor", EA[:, 1024:1536],
                     EA[:, 1536:2048], EA[:, 512:1024], op=OP.add, sem=VE)
                c.wait("sync", VE)
                c.dma("sync", c.t["x1"][tok0:tok0 + 128, ds * 512:(ds + 1) * 512],
                      EA[:, 1024:1536], sem=ST)
                c.wait("scalar", ST)


L1QKV_LOGITS = True
L1QKV_PROJ = True
L1QKV_PARTS = {"mm", "copy", "store"}
DBG_CW = False


def ph_l1qkv(c, sm):
    XT, WA, QSB, EA = c.sb["XT"], c.sb["WA"], c.sb["QSB"], c.sb["EA"]
    GW, GWT = c.sb["GW"], c.sb["GWT"]
    LD, GLD, MM, VE, SC, ST = (sm["LD"], sm["GLD"], sm["MM"], sm["VE"],
                               sm["SC"], sm["ST"])
    PF = [c.ps["PF0"], c.ps["PF1"], c.ps["PF2"]]
    PF3 = c.ps["PF3"]
    IDB = c.sb["IDB"]
    wqkv = c.t["wqkv_bf"]
    outs = {0: "q1_tok", 1: "k1_tok", 2: "v1_tok"}
    c.op("vector", "memset", c.sb["BIG"][0:8, 0:2048], 0, sem=VE)
    c.dma("gpsimd", c.t["gate_pad"][0:8, :], c.t["gatew"], sem=GLD)
    c.wait("sync", VE)
    c.wait("sync", GLD)
    c.dma("sync", c.t["gate_pad"][8:16, :], c.sb["BIG"][0:8, 0:2048], sem=ST)
    c.wait("sync", ST)
    for dc in range(16):
        c.dma("sync", GWT[:, dc * 16:(dc + 1) * 16],
              c.t["gate_pad"][:, dc * 128:(dc + 1) * 128],
              sem=LD, transpose=True)
    for tg in range(4):
        c.wait("sync", MM)
        _load_xnT(c, sm, c.t["xn1_tok"], tg)
        c.wait("tensor", LD)
        c.wait("tensor", VE)
        for qt in (range(4) if L1QKV_LOGITS else []):
            if "mm" in L1QKV_PARTS:
                c.wait("tensor", VE)
                for dc in range(16):
                    c.mm(PF3[:, 0:8],
                         XT[:, dc * 512 + qt * 128:dc * 512 + qt * 128 + 128],
                         GWT[:, dc * 16:dc * 16 + 8],
                         start=(dc == 0), stop=(dc == 15),
                         sem=(MM if dc == 15 else None))
            if "copy" in L1QKV_PARTS:
                c.wait("vector", MM)
                c.op("vector", "tensor_copy", EA[:, qt * 8:qt * 8 + 8],
                     PF3[:, 0:8], sem=VE)
        c.wait("sync", VE)
        for qt in (range(4) if (L1QKV_LOGITS and "store" in L1QKV_PARTS)
                   else []):
            c.dma("sync", c.t["logits"][(tg * 4 + qt) * 128:(tg * 4 + qt + 1) * 128, :],
                  EA[:, qt * 8:qt * 8 + 8], sem=ST,
                  allow_slow_non_contiguous=True)
        for es in (range(12) if L1QKV_PROJ else []):
            wbuf = WA if es % 2 == 0 else c.sb["WB"]
            c.wait("sync", MM, c.lu.get(("l1qw", es % 2)))
            for dc in range(16):
                c.dma("sync", wbuf[:, dc * 512:(dc + 1) * 512],
                      wqkv[es * 512:(es + 1) * 512, dc * 128:(dc + 1) * 128],
                      sem=LD, transpose=True)
            ld_here = LD.n
            for qt in range(4):
                pf = PF[qt % 3]
                c.wait("tensor", LD, ld_here)
                c.wait("tensor", SC, c.lu.get(("l1qp", qt % 3)))
                for dc in range(16):
                    c.mm(pf[:, :],
                         XT[:, dc * 512 + qt * 128:dc * 512 + qt * 128 + 128],
                         wbuf[:, dc * 512:(dc + 1) * 512],
                         start=(dc == 0), stop=(dc == 15),
                         sem=(MM if dc == 15 else None))
                c.lu[("l1qw", es % 2)] = MM.n
                qsb = QSB if qt % 2 == 0 else c.sb["SGB"]
                c.wait("scalar", MM)
                c.wait("scalar", ST, c.lu.get(("l1qs", qt % 2)))
                c.op("scalar", "activation", qsb[:, :], pf[:, :], AF.Copy,
                     sem=SC)
                c.lu[("l1qp", qt % 3)] = SC.n
                c.wait("sync", SC)
                tok0 = tg * 512 + qt * 128
                c.dma("sync", c.t[outs[es // 4]][tok0:tok0 + 128,
                                                 (es % 4) * 512:(es % 4) * 512 + 512],
                      qsb[:, :], sem=ST)
                c.lu[("l1qs", qt % 2)] = ST.n


def ph_l1attn(c, sm):
    BIG, MED, EA = c.sb["BIG"], c.sb["MED"], c.sb["EA"]
    ST1, ST2 = c.sb["ST1"], c.sb["ST2"]
    IDB, CMASK = c.sb["IDB"], c.sb["CMASK"]
    ROWC, ROWD, ROWE = c.sb["ROWC"], c.sb["ROWD"], c.sb["ROWE"]
    NLB, GB, BB = c.sb["NLB"], c.sb["GB"], c.sb["BB"]
    LD, MM, VE, SC, ST = sm["LD"], sm["MM"], sm["VE"], sm["SC"], sm["ST"]
    PSB = c.ps["PSB"]
    PF0, PF1, PF2, PF3 = c.ps["PF0"], c.ps["PF1"], c.ps["PF2"], c.ps["PF3"]
    qT = BIG[:, 0:2048]
    kT = BIG[:, 2048:4096]
    vh = BIG[:, 4096:6144]
    v2h = BIG[:, 6144:8192]
    a1 = BIG[:, 8192:10240]
    a2 = BIG[:, 10240:12288]
    at1 = BIG[:, 12288:14336]
    at2 = BIG[:, 14336:16384]
    otb = BIG[:, 16384:18432]

    c.dma("sync", IDB[:, :], c.t["c_idb"], sem=LD)
    c.dma("sync", CMASK[:, :], c.t["c_cmask"], sem=LD)
    c.dma("sync", ROWC[:, :], c.t["c_ones"], sem=LD)
    c.dma("sync", EA[0:1, 0:128], c.t["lq1"], sem=LD)
    c.dma("sync", EA[0:1, 128:256], c.t["lk1"], sem=LD)
    c.dma("sync", EA[0:1, 256:384], c.t["lq2"], sem=LD)
    c.dma("sync", EA[0:1, 384:512], c.t["lk2"], sem=LD)
    c.wait("vector", LD)
    c.op("vector", "tensor_tensor", EA[0:1, 512:640], EA[0:1, 0:128],
         EA[0:1, 128:256], op=OP.mult)
    c.op("vector", "tensor_tensor", EA[0:1, 640:768], EA[0:1, 256:384],
         EA[0:1, 384:512], op=OP.mult, sem=VE)
    c.wait("scalar", VE)
    c.op("scalar", "activation", EA[0:1, 768:896], EA[0:1, 512:640], AF.Exp)
    c.op("scalar", "activation", EA[0:1, 896:1024], EA[0:1, 640:768], AF.Exp,
         sem=SC)
    c.wait("vector", SC)
    c.op("vector", "tensor_tensor", ROWD[:, :], EA[0:1, 896:1024],
         EA[0:1, 768:896], op=OP.subtract)
    c.op("vector", "tensor_scalar", ROWD[:, :], ROWD[:, :], LAMBDA_INIT, None,
         op0=OP.subtract, sem=VE)
    c.wait("tensor", VE)
    c.mm(PF3[:, 0:128], ROWC[0:1, 0:128], ROWD[0:1, :], start=True, stop=True,
         sem=MM)
    c.wait("vector", MM)
    c.op("vector", "tensor_copy", NLB[:, :], PF3[:, 0:128], sem=VE)

    for h in range(16):
        c.wait("sync", VE)
        c.wait("sync", MM)
        c.dma("sync", ROWD[:, :], c.t["gng"][h:h + 1, :], sem=LD)
        c.dma("sync", ROWE[:, :], c.t["gnb"][h:h + 1, :], sem=LD)
        c.wait("vector", LD)
        c.op("vector", "tensor_scalar", ROWD[:, :], ROWD[:, :],
             1.0 - LAMBDA_INIT, None, op0=OP.mult)
        c.op("vector", "tensor_scalar", ROWE[:, :], ROWE[:, :],
             1.0 - LAMBDA_INIT, None, op0=OP.mult, sem=VE)
        c.wait("tensor", VE)
        c.wait("tensor", SC)
        c.mm(PF3[:, 0:128], ROWC[0:1, 0:128], ROWD[0:1, :], start=True,
             stop=True)
        c.mm(PF3[:, 128:256], ROWC[0:1, 0:128], ROWE[0:1, :], start=True,
             stop=True, sem=MM)
        c.wait("vector", MM)
        c.op("vector", "tensor_copy", GB[:, :], PF3[:, 0:128])
        c.op("vector", "tensor_copy", BB[:, :], PF3[:, 128:256], sem=VE)
        c.dma("sync", qT, c.t["q1_tok"][:, h * 128:(h + 1) * 128], sem=LD,
              transpose=True)
        c.dma("sync", kT, c.t["k1_tok"][:, h * 128:(h + 1) * 128], sem=LD,
              transpose=True)
        c.dma("sync", vh.rearrange("p (k d) -> p k d", k=16),
              c.t["v1_tok"][:, h * 128:(h + 1) * 128]
              .rearrange("(k p) d -> p k d", p=128), sem=LD)
        c.wait("vector", LD)
        c.op("vector", "tensor_tensor",
             v2h.rearrange("p (k d) -> p k d", k=16),
             vh.rearrange("p (k d) -> p k d", k=16),
             NLB[:, :].unsqueeze(1).broadcast_to([128, 16, 128]),
             op=OP.mult, sem=VE)
        for qt in range(16):
            nk = qt + 1
            kend = nk * 128
            for half, (qs, ks, ab) in enumerate(
                    [(qT[0:64, qt * 128:(qt + 1) * 128], kT, a1),
                     (qT[64:128, qt * 128:(qt + 1) * 128], kT, a2)]):
                c.wait("tensor", LD)
                c.wait("tensor", VE)
                c.wait("tensor", SC)
                nch = (kend + 511) // 512
                for ch in range(nch):
                    n = min(512, kend - ch * 512)
                    ksl = (kT[0:64, ch * 512:ch * 512 + n] if half == 0
                           else kT[64:128, ch * 512:ch * 512 + n])
                    c.mm(PSB[:, ch * 512:ch * 512 + n], qs, ksl,
                         start=True, stop=True,
                         sem=(MM if ch == nch - 1 else None))
                c.wait("vector", MM)
                c.op("vector", "tensor_tensor", PSB[:, kend - 128:kend],
                     PSB[:, kend - 128:kend], CMASK[:, :], op=OP.add)
                c.op("vector", "reduce_max", ST1[:, 0:1], PSB[:, 0:kend],
                     axis=AX.X, negate=True)
                c.op("vector", "tensor_scalar", ST1[:, 1:2], ST1[:, 0:1], SC1,
                     None, op0=OP.mult, sem=VE)
                c.wait("scalar", VE)
                c.wait("scalar", MM)
                c.op("scalar", "activation", ab[:, 0:kend], PSB[:, 0:kend],
                     AF.Exp, bias=ST1[:, 1:2], scale=SC1,
                     accum_out=ST1[:, 2:3])
                c.op("scalar", "activation", ST1[:, 4:5], ST1[:, 2:3],
                     AF.Ln, sem=SC)
                c.wait("scalar", SC)
                c.op("scalar", "activation", ST1[:, 3:4], ST1[:, 4:5], AF.Exp,
                     scale=-1.0, sem=SC)
                c.wait("vector", SC)
                c.op("vector", "tensor_scalar", ab[:, 0:kend], ab[:, 0:kend],
                     ST1[:, 3:4], None, op0=OP.mult, sem=VE)
            pt0 = PF0[:, :].bitcast(BF16)
            pt1 = PF1[:, :].bitcast(BF16)
            c.wait("tensor", VE)
            for k in range(nk):
                sl = (k % 8) * 128
                c.op("tensor", "transpose", pt0[:, sl:sl + 128],
                     a1[:, k * 128:(k + 1) * 128], IDB[:, :])
                c.op("tensor", "transpose", pt1[:, sl:sl + 128],
                     a2[:, k * 128:(k + 1) * 128], IDB[:, :],
                     sem=(MM if (k % 8 == 7 or k == nk - 1) else None))
                if k % 8 == 7 or k == nk - 1:
                    c.wait("vector", MM)
                    base = (k // 8) * 8 * 128
                    w = sl + 128 - (0 if True else 0)
                    c.op("vector", "tensor_copy",
                         at1[:, base:base + sl + 128], pt0[:, 0:sl + 128])
                    c.op("vector", "tensor_copy",
                         at2[:, base:base + sl + 128], pt1[:, 0:sl + 128],
                         sem=VE)
                    if k < nk - 1:
                        c.wait("tensor", VE)
            c.wait("tensor", VE)
            c.wait("tensor", SC)
            for k in range(nk):
                c.mm(PF2[:, 0:128], at1[:, k * 128:(k + 1) * 128],
                     vh[:, k * 128:(k + 1) * 128], start=(k == 0), stop=False)
            for k in range(nk):
                c.mm(PF2[:, 0:128], at2[:, k * 128:(k + 1) * 128],
                     v2h[:, k * 128:(k + 1) * 128], start=False,
                     stop=(k == nk - 1), sem=(MM if k == nk - 1 else None))
            c.wait("vector", MM)
            c.wait("scalar", MM)
            c.wait("scalar", VE)
            c.op("vector", "reduce_sum", ST2[:, 0:1], PF2[:, 0:128], axis=AX.X)
            c.op("scalar", "activation", EA[:, 0:128], PF2[:, 0:128],
                 AF.Square, sem=SC)
            c.op("vector", "tensor_scalar", ST2[:, 1:2], ST2[:, 0:1], 1.0 / 128,
                 None, op0=OP.mult)
            c.op("vector", "tensor_scalar", ST2[:, 2:3], ST2[:, 1:2],
                 ST2[:, 1:2], None, op0=OP.mult)
            c.wait("vector", SC)
            c.op("vector", "reduce_sum", ST2[:, 3:4], EA[:, 0:128], axis=AX.X)
            c.op("vector", "tensor_scalar", ST2[:, 4:5], ST2[:, 3:4],
                 1.0 / 128, ST2[:, 2:3], op0=OP.mult, op1=OP.subtract, sem=VE)
            c.wait("scalar", VE)
            c.op("scalar", "activation", ST2[:, 5:6], ST2[:, 4:5], AF.Ln,
                 bias=1e-5, sem=SC)
            c.wait("scalar", SC)
            c.op("scalar", "activation", ST2[:, 6:7], ST2[:, 5:6], AF.Exp,
                 scale=-0.5, sem=SC)
            c.wait("vector", SC)
            c.op("vector", "tensor_scalar", EA[:, 128:256], PF2[:, 0:128],
                 ST2[:, 1:2], ST2[:, 6:7], op0=OP.subtract, op1=OP.mult)
            c.op("vector", "tensor_tensor", EA[:, 256:384], EA[:, 128:256],
                 GB[:, :], op=OP.mult)
            c.op("vector", "tensor_tensor", c.sb["QSB"][:, 0:128],
                 EA[:, 256:384], BB[:, :], op=OP.add, sem=VE)
            c.wait("tensor", VE)
            pt0 = PF3[:, :].bitcast(BF16)
            c.op("tensor", "transpose", pt0[:, 0:128], c.sb["QSB"][:, 0:128],
                 IDB[:, :], sem=MM)
            c.wait("vector", MM)
            c.op("vector", "tensor_copy", otb[:, qt * 128:(qt + 1) * 128],
                 pt0[:, 0:128], sem=VE)
        c.wait("sync", VE)
        c.dma("sync", c.t["oT1"][h * 128:(h + 1) * 128, :], otb, sem=ST)
        c.wait("vector", ST)


def ph_l1moe(c, sm):
    XT, WA, WB, BIG, MED, EA, SGB = (c.sb["XT"], c.sb["WA"], c.sb["WB"],
                                     c.sb["BIG"], c.sb["MED"], c.sb["EA"],
                                     c.sb["SGB"])
    CWT, LT, ST1 = c.sb["CWT"], c.sb["LT"], c.sb["ST1"]
    LD, GLD, MM, VE, SC, ST = (sm["LD"], sm["GLD"], sm["MM"], sm["VE"],
                               sm["SC"], sm["ST"])
    PSB = c.ps["PSB"]
    PF0, PF1 = c.ps["PF0"], c.ps["PF1"]
    accf = MED[:, :].bitcast(F32)
    hE = BIG[:, 0:5632]
    sg = BIG[:, 5632:11264]
    for tt in range(16):
        c.wait("sync", VE)
        c.dma("sync", LT[:, 0:8], c.t["logits"][tt * 128:(tt + 1) * 128, :],
              sem=LD, allow_slow_non_contiguous=True)
        c.wait("vector", LD)
        c.op("vector", "reduce_max", ST1[:, 0:1], LT[:, 0:8], axis=AX.X)
        c.op("vector", "tensor_scalar", LT[:, 8:16], LT[:, 0:8], ST1[:, 0:1],
             None, op0=OP.is_ge)
        c.op("vector", "scalar_tensor_tensor", EA[:, 0:8], LT[:, 8:16], NEG,
             LT[:, 0:8], op0=OP.mult, op1=OP.add)
        c.op("vector", "reduce_max", ST1[:, 1:2], EA[:, 0:8], axis=AX.X)
        c.op("vector", "tensor_scalar", EA[:, 8:16], EA[:, 0:8], ST1[:, 1:2],
             None, op0=OP.is_ge)
        c.op("vector", "tensor_scalar", ST1[:, 2:3], ST1[:, 1:2], ST1[:, 0:1],
             None, op0=OP.subtract, sem=VE)
        c.wait("scalar", VE)
        c.op("scalar", "activation", ST1[:, 5:6], ST1[:, 2:3], AF.Sigmoid,
             scale=-1.0)
        c.op("scalar", "activation", ST1[:, 6:7], ST1[:, 2:3], AF.Sigmoid,
             sem=SC)
        c.wait("vector", SC)
        c.op("vector", "tensor_scalar", EA[:, 16:24], LT[:, 8:16], ST1[:, 5:6],
             None, op0=OP.mult)
        c.op("vector", "scalar_tensor_tensor", CWT[:, tt * 8:(tt + 1) * 8],
             EA[:, 8:16], ST1[:, 6:7], EA[:, 16:24],
             op0=OP.mult, op1=OP.add, sem=VE)
        if DBG_CW:
            c.wait("sync", VE)
            c.dma("sync", c.t["dbg_cw"][tt * 128:(tt + 1) * 128, :],
                  CWT[:, tt * 8:(tt + 1) * 8], sem=ST,
                  allow_slow_non_contiguous=True)
    for sp in range(4):
        c.wait("sync", MM)
        _load_xnT(c, sm, c.t["xn1_tok"], sp)
        c.wait("vector", SC)
        for e in range(NE):
            for ib in range(22):
                slot = ib % 4
                wc = WA[:, slot * 2048:(slot + 1) * 2048]
                c.wait("sync", MM, c.lu.get(("moew1", slot)))
                c.dma("sync", wc.rearrange("p (k i) -> p k i", k=16),
                      c.t["w1_bf"][e * D:(e + 1) * D, ib * 128:(ib + 1) * 128]
                      .rearrange("(k p) i -> p k i", p=128), sem=LD)
                ld_here = LD.n
                pf = PF0 if ib % 2 == 0 else PF1
                c.wait("tensor", LD, ld_here)
                if ib >= 2:
                    c.wait("tensor", SC if ib % 2 == 0 else VE)
                for hc in range(16):
                    c.mm(pf[:, :], wc[:, hc * 128:(hc + 1) * 128],
                         XT[:, hc * 512:(hc + 1) * 512],
                         start=(hc == 0), stop=(hc == 15),
                         sem=(MM if hc == 15 else None))
                c.lu[("moew1", slot)] = MM.n
                if ib < 11:
                    c.wait("scalar", MM)
                    c.op("scalar", "activation", sg[:, ib * 512:(ib + 1) * 512],
                         pf[:, :], AF.Silu, sem=SC)
                else:
                    c.wait("vector", MM)
                    c.wait("vector", SC)
                    c.op("vector", "tensor_tensor",
                         hE[:, (ib - 11) * 512:(ib - 10) * 512],
                         sg[:, (ib - 11) * 512:(ib - 10) * 512], pf[:, :],
                         op=OP.mult, sem=VE)
            for ds in range(4):
                for ic in range(11):
                    slot = (ds * 11 + ic) % 8
                    wslot = WB[:, slot * 512:slot * 512 + 512]
                    c.wait("sync", MM, c.lu.get(("moew2", slot)))
                    c.dma("sync", wslot,
                          c.t["w2_bf"][e * DI + ic * 128:e * DI + (ic + 1) * 128,
                                       ds * 512:(ds + 1) * 512], sem=LD)
                    ld_here = LD.n
                    c.wait("tensor", LD, ld_here)
                    if ic == 0:
                        c.wait("tensor", VE)
                    for st in range(4):
                        c.mm(PSB[:, st * 512:(st + 1) * 512],
                             hE[:, ic * 512 + st * 128:ic * 512 + st * 128 + 128],
                             wslot, start=(ic == 0), stop=(ic == 10),
                             sem=(MM if st == 3 else None))
                    c.lu[("moew2", slot)] = MM.n
                c.wait("vector", MM)
                for st in range(4):
                    acc = accf[:, st * 2048 + ds * 512:st * 2048 + ds * 512 + 512]
                    cwcol = CWT[:, (sp * 4 + st) * 8 + e:(sp * 4 + st) * 8 + e + 1]
                    if e == 0:
                        c.op("vector", "tensor_scalar", acc,
                             PSB[:, st * 512:(st + 1) * 512], cwcol, None,
                             op0=OP.mult, sem=(VE if st == 3 else None))
                    else:
                        c.op("vector", "scalar_tensor_tensor", acc,
                             PSB[:, st * 512:(st + 1) * 512], cwcol, acc,
                             op0=OP.mult, op1=OP.add,
                             sem=(VE if st == 3 else None))
        c.lu[("moesh", 0)] = c.lu[("moesh", 1)] = MM.n
        for ib in range(11):
            base = 0 if ib % 2 == 0 else 4096
            c.wait("sync", MM, c.lu.get(("moesh", ib % 2)))
            for dc in range(16):
                c.dma("sync", WA[:, base + dc * 128:base + (dc + 1) * 128],
                      c.t["sw1_bf"][ib * 128:(ib + 1) * 128,
                                    dc * 128:(dc + 1) * 128],
                      sem=LD, transpose=True)
                c.dma("sync", WA[:, base + 2048 + dc * 128:base + 2048 + (dc + 1) * 128],
                      c.t["sw2_bf"][ib * 128:(ib + 1) * 128,
                                    dc * 128:(dc + 1) * 128],
                      sem=LD, transpose=True)
            ld_here = LD.n
            c.wait("tensor", LD, ld_here)
            if ib >= 1:
                c.wait("tensor", VE)
            for dc in range(16):
                c.mm(PF0[:, :], WA[:, base + dc * 128:base + (dc + 1) * 128],
                     XT[:, dc * 512:(dc + 1) * 512],
                     start=(dc == 0), stop=(dc == 15))
            for dc in range(16):
                c.mm(PF1[:, :], WA[:, base + 2048 + dc * 128:base + 2048 + (dc + 1) * 128],
                     XT[:, dc * 512:(dc + 1) * 512],
                     start=(dc == 0), stop=(dc == 15),
                     sem=(MM if dc == 15 else None))
            c.lu[("moesh", ib % 2)] = MM.n
            c.wait("scalar", MM)
            c.op("scalar", "activation", SGB[:, :], PF0[:, :], AF.Silu, sem=SC)
            c.wait("vector", SC)
            c.wait("vector", MM)
            c.op("vector", "tensor_tensor", hE[:, ib * 512:(ib + 1) * 512],
                 SGB[:, :], PF1[:, :], op=OP.mult, sem=VE)
        for ds in range(4):
            for ic in range(11):
                slot = (ds * 11 + ic) % 8
                wslot = WB[:, slot * 512:slot * 512 + 512]
                c.wait("sync", MM, c.lu.get(("moew2", slot)))
                c.dma("sync", wslot,
                      c.t["sw3_bf"][ds * 512:(ds + 1) * 512,
                                    ic * 128:(ic + 1) * 128],
                      sem=LD, transpose=True)
                ld_here = LD.n
                c.wait("tensor", LD, ld_here)
                if ic == 0:
                    c.wait("tensor", VE)
                for st in range(4):
                    c.mm(PSB[:, st * 512:(st + 1) * 512],
                         hE[:, ic * 512 + st * 128:ic * 512 + st * 128 + 128],
                         wslot, start=(ic == 0), stop=(ic == 10),
                         sem=(MM if st == 3 else None))
                c.lu[("moew2", slot)] = MM.n
            c.wait("vector", MM)
            for st in range(4):
                acc = accf[:, st * 2048 + ds * 512:st * 2048 + ds * 512 + 512]
                c.op("vector", "scalar_tensor_tensor", acc,
                     PSB[:, st * 512:(st + 1) * 512], 1.0, acc,
                     op0=OP.mult, op1=OP.add, sem=(VE if st == 3 else None))
        for st in range(4):
            for ds in range(4):
                tok0 = sp * 512 + st * 128
                acc = accf[:, st * 2048 + ds * 512:st * 2048 + ds * 512 + 512]
                c.wait("scalar", VE)
                c.wait("scalar", ST)
                c.op("scalar", "activation", EA[:, 0:512], acc, AF.Silu, sem=SC)
                c.wait("gpsimd", VE)
                c.dma("gpsimd", EA[:, 512:1024],
                      c.t["s1"][tok0:tok0 + 128, ds * 512:(ds + 1) * 512],
                      sem=GLD)
                c.dma("gpsimd", EA[:, 1024:1536],
                      c.t["x1"][tok0:tok0 + 128, ds * 512:(ds + 1) * 512],
                      sem=GLD)
                c.wait("vector", SC)
                c.wait("vector", GLD)
                c.wait("vector", ST)
                c.op("vector", "tensor_tensor", EA[:, 1536:2048], EA[:, 0:512],
                     EA[:, 512:1024], op=OP.mult)
                c.op("vector", "tensor_tensor", EA[:, 1024:1536],
                     EA[:, 1536:2048], EA[:, 1024:1536], op=OP.add, sem=VE)
                c.wait("sync", VE)
                c.dma("sync", c.t["out"][tok0:tok0 + 128,
                                         ds * 512:(ds + 1) * 512],
                      EA[:, 1024:1536], sem=ST)
    c.wait("sync", sm["ST"])
    c.wait("gpsimd", sm["ST"])


PHASE_FNS = {
    "conv": ph_conv,
    "quant": ph_quant,
    "quantgu": lambda c, sm: _quant_one(c, sm, c.t["wgu"], c.t["wgu_hat"],
                                        2 * HF, D),
    "ln0": lambda c, sm: ph_ln(c, sm, "x", "xn_tok"),
    "l0qkv": ph_l0qkv,
    "l0attn": ph_l0attn,
    "l0wo": lambda c, sm: ph_wo(c, sm, "oT0", "wo_bf", "bo", "s0"),
    "l0ffn": ph_l0ffn,
    "ln1": lambda c, sm: ph_ln(c, sm, "x1", "xn1_tok"),
    "l1qkv": ph_l1qkv,
    "l1attn": ph_l1attn,
    "l1wo": lambda c, sm: ph_wo(c, sm, "oT1", "wo1_bf", None, "s1"),
    "l1moe": ph_l1moe,
}


def build_nc(phases=PHASES_ALL, feed=(), want=()):
    nc = bass.Bass("TRN2", target_bir_lowering=False)
    c = Ctx(nc)
    declare_tensors(c, phases, set(feed), set(want))
    sm = {k: c.sem(k) for k in ("LD", "GLD", "MM", "VE", "SC", "ST")}
    with contextlib.ExitStack() as stk:
        alloc_onchip(c, stk)
        CONSTF = c.sb["CONSTF"]
        cvals = (1e-5, SC0, SC1, -1.0, -0.5)
        for i, val in enumerate(cvals):
            c.op("vector", "memset", CONSTF[:, i:i + 1], val,
                 sem=(sm["VE"] if i == len(cvals) - 1 else None))
            nc.const_aps.register(F32, val, CONSTF[:, i:i + 1])
        c.fence()
        for ph in phases:
            PHASE_FNS[ph](c, sm)
            c.fence()
        c.wait("sync", sm["ST"])
        for s in c.sems:
            s.h = stk.enter_context(nc.semaphore(s.name))
        blk = stk.enter_context(nc.Block())
        for eng in ENGS:
            ops = c.ops[eng]
            if not ops:
                continue
            getattr(blk, eng)(lambda e, ops=ops: [f(e) for f in ops])
    return nc


def kernel(**inputs):
    shared = prep_inputs(inputs)
    x = np.ascontiguousarray(inputs["L0_attn_hidden_states"], dtype=np.float32)
    nc = build_nc()
    in_maps = []
    for core in range(8):
        m = dict(shared)
        m["x"] = np.ascontiguousarray(x[core])
        in_maps.append(m)
    res = run_bass_kernel_spmd(nc, in_maps, core_ids=list(range(8)))
    return np.stack([r["out"] for r in res.results], axis=0)


# version 04 (final)
# speedup vs baseline: 1.0084x; 1.0084x over previous
import math
import contextlib
import numpy as np
import ml_dtypes
import concourse.bass as bass
import concourse.mybir as mybir
from concourse.alu_op_type import AluOpType as OP
from concourse.bass_utils import run_bass_kernel_spmd

F32 = mybir.dt.float32
BF16 = mybir.dt.bfloat16
AF = mybir.ActivationFunctionType
AX = mybir.AxisListType

S = 2048
D = 2048
H0 = 16
DH0 = 128
NSEL = 717
KPAD = 768
HF = 8192
H1 = 16
DH1 = 128
NE = 8
DI = 1408
SC0 = 1.0 / math.sqrt(128.0)
SC1 = 64.0 ** -0.5
LAMBDA_INIT = float(0.8 - 0.6 * math.exp(-3.6))
RMS_EPS = float(np.finfo(np.float32).eps)
MAGIC = 12582912.0
NEG = -1.0e30
BF16_1EM8 = float(np.float32(np.asarray(1e-8, ml_dtypes.bfloat16)))

ENGS = ("tensor", "vector", "scalar", "gpsimd", "sync")

KSEL_TILES = [(0, 128, 0), (128, 77, 128), (1536, 128, 205), (1664, 128, 333),
              (1792, 128, 461), (1920, 128, 589)]

PHASES_ALL = ("conv", "quant", "ln0", "l0qkv", "l0attn", "l0wo", "l0ffn",
              "ln1", "l1qkv", "l1attn", "l1wo", "l1moe")

SCRATCH = {
    "xn_tok":  ([S, D], BF16),
    "q0_tok":  ([S, D], BF16),
    "k0_tok":  ([KPAD, D], BF16),
    "v0_tok":  ([KPAD, D], BF16),
    "oT0":     ([D, S], BF16),
    "s0":      ([S, D], F32),
    "x1":      ([S, D], F32),
    "xn1_tok": ([S, D], BF16),
    "q1_tok":  ([S, D], BF16),
    "k1_tok":  ([S, D], BF16),
    "v1_tok":  ([S, D], BF16),
    "logits":  ([S, 8], F32),
    "oT1":     ([D, S], BF16),
    "s1":      ([S, D], F32),
    "wq_bf":   ([D, D], BF16),
    "wk_bf":   ([D, D], BF16),
    "wv_bf":   ([D, D], BF16),
    "wo_bf":   ([D, D], BF16),
    "wgu_hat": ([2 * HF, D], BF16),
    "wp_hat":  ([D, HF], BF16),
    "wqkv_bf": ([3 * D, D], BF16),
    "wo1_bf":  ([D, D], BF16),
    "sw1_bf":  ([DI, D], BF16),
    "sw2_bf":  ([DI, D], BF16),
    "sw3_bf":  ([D, DI], BF16),
    "w1_bf":   ([NE * D, 2 * DI], BF16),
    "w2_bf":   ([NE * DI, D], BF16),
    "gate_pad": ([16, D], BF16),
    "dbg_cw":  ([S, 8], F32),
}

MODINS = {
    "x":      ([S, D], F32),
    "wq":     ([D, D], F32),
    "wk":     ([D, D], F32),
    "wv":     ([D, D], F32),
    "wo":     ([D, D], F32),
    "bq":     ([1, D], F32),
    "bk":     ([1, D], F32),
    "bv":     ([1, D], F32),
    "bo":     ([1, D], F32),
    "wgu":    ([2 * HF, D], F32),
    "wp":     ([D, HF], F32),
    "wqkv":   ([3 * D, D], F32),
    "wo1":    ([D, D], F32),
    "lq1":    ([1, 128], F32),
    "lk1":    ([1, 128], F32),
    "lq2":    ([1, 128], F32),
    "lk2":    ([1, 128], F32),
    "gng":    ([16, 128], F32),
    "gnb":    ([16, 128], F32),
    "gatew":  ([8, D], F32),
    "w1":     ([NE * D, 2 * DI], F32),
    "w2":     ([NE * DI, D], F32),
    "sw1":    ([DI, D], F32),
    "sw2":    ([DI, D], F32),
    "sw3":    ([D, DI], F32),
    "c_ones": ([1, 512], F32),
    "c_idf":  ([128, 128], F32),
    "c_idb":  ([128, 128], BF16),
    "c_cmask": ([128, 128], F32),
    "c_scq":  ([S, 512], F32),
}


class Sem:
    __slots__ = ("name", "n", "h")

    def __init__(self, name):
        self.name, self.n, self.h = name, 0, None


CHAIN_ENGS = ("vector", "scalar", "gpsimd")


class Ctx:

    def __init__(self, nc):
        self.nc = nc
        self.ops = {e: [] for e in ENGS}
        self.sems = []
        self.t = {}
        self.sb = {}
        self.ps = {}
        self.auto = {e: self.sem("AUTO_" + e[0]) for e in CHAIN_ENGS}
        self.last = {}
        self.lu = {}
        self.nofence = []

    def sem(self, name, fenced=True):
        s = Sem(name)
        if fenced:
            self.sems.append(s)
        else:
            self.nofence.append(s)
        return s

    def emit(self, eng, fn):
        self.ops[eng].append(fn)

    def op(self, eng, meth, *args, sem=None, inc=1, chain=True, **kw):
        is_dma = meth.startswith("dma_start") or meth == "indirect_dma_start"
        chaining = eng in CHAIN_ENGS and chain and not is_dma
        if chaining:
            prev = self.last.get(eng)
            if prev is not None:
                self.wait(eng, prev[0], prev[1])
            if sem is None:
                sem, inc = self.auto[eng], 1
        if sem is not None:
            sem.n += inc

        def f(e, meth=meth, args=args, kw=kw, s=sem, inc=inc):
            i = getattr(e, meth)(*args, **kw)
            if s is not None:
                i.then_inc(s.h, inc)
        self.emit(eng, f)
        if chaining:
            self.last[eng] = (sem, sem.n)

    def wait(self, eng, sem, at=None):
        v = sem.n if at is None else at
        if v is None or v <= 0:
            return
        self.emit(eng, lambda e, s=sem, v=v: e.wait_ge(s.h, v))

    def dma(self, q, dst, src, sem, **kw):
        self.op(q, "dma_start", dst, src, sem=sem, inc=16, **kw)

    def mm(self, out, lhsT, rhs, start, stop, sem=None, **kw):
        kw.setdefault("skip_group_check", True)
        self.op("tensor", "matmul", out, lhsT, rhs, sem=sem,
                start=start, stop=stop, **kw)

    def fence(self):
        for eng in ENGS:
            for s in self.sems:
                self.wait(eng, s)


def prep_inputs(inputs):
    g = {}
    g["wq"] = inputs["L0_attn_Wq"]
    g["wk"] = inputs["L0_attn_Wk"]
    g["wv"] = inputs["L0_attn_Wv"]
    g["wo"] = inputs["L0_attn_Wo"]
    g["bq"] = inputs["L0_attn_bq"].reshape(1, D)
    g["bk"] = inputs["L0_attn_bk"].reshape(1, D)
    g["bv"] = inputs["L0_attn_bv"].reshape(1, D)
    g["bo"] = inputs["L0_attn_bo"].reshape(1, D)
    g["wgu"] = inputs["L0_ffn_w_gate_up"]
    g["wp"] = inputs["L0_ffn_w_proj"]
    g["wqkv"] = inputs["L1_attn_W_qkv"]
    g["wo1"] = inputs["L1_attn_W_o"]
    g["lq1"] = inputs["L1_attn_lambda_q1"].reshape(1, 128)
    g["lk1"] = inputs["L1_attn_lambda_k1"].reshape(1, 128)
    g["lq2"] = inputs["L1_attn_lambda_q2"].reshape(1, 128)
    g["lk2"] = inputs["L1_attn_lambda_k2"].reshape(1, 128)
    g["gng"] = inputs["L1_attn_gn_gamma"]
    g["gnb"] = inputs["L1_attn_gn_beta"]
    g["gatew"] = inputs["L1_ffn_gate_w"]
    g["w1"] = np.ascontiguousarray(inputs["L1_ffn_w1"].reshape(NE * D, 2 * DI))
    g["w2"] = np.ascontiguousarray(inputs["L1_ffn_w2"].reshape(NE * DI, D))
    g["sw1"] = inputs["L1_ffn_sw1"]
    g["sw2"] = inputs["L1_ffn_sw2"]
    g["sw3"] = inputs["L1_ffn_sw3"]
    g["c_ones"] = np.ones((1, 512), np.float32)
    g["c_idf"] = np.eye(128, dtype=np.float32)
    g["c_idb"] = np.eye(128).astype(ml_dtypes.bfloat16)
    cm = np.zeros((128, 128), np.float32)
    cm[np.triu_indices(128, 1)] = NEG
    g["c_cmask"] = cm
    theta = 1.0 / 10000.0 ** (np.arange(0, 128, 2, dtype=np.float32) / 128.0)
    sinu = np.arange(S, dtype=np.float32)[:, None] * theta[None, :]
    sin = np.sin(sinu).astype(np.float32)
    cos = np.cos(sinu).astype(np.float32)
    g["c_scq"] = np.concatenate([np.tile(sin, (1, 4)), np.tile(cos, (1, 4))],
                                axis=1)
    for k in g:
        want = MODINS[k]
        assert list(g[k].shape) == want[0], (k, g[k].shape, want)
        g[k] = np.ascontiguousarray(g[k])
    return g


def declare_tensors(c, phases, feed, want):
    nc = c.nc
    for name, (shape, dt) in MODINS.items():
        c.t[name] = nc.dram_tensor(name, shape, dt, kind="ExternalInput").ap()
    for name, (shape, dt) in SCRATCH.items():
        if name in feed:
            kind = "ExternalInput"
        elif name in want:
            kind = "ExternalOutput"
        else:
            kind = "Internal"
        c.t[name] = nc.dram_tensor(name, shape, dt, kind=kind).ap()
    c.t["out"] = nc.dram_tensor("out", [S, D], F32, kind="ExternalOutput").ap()


def alloc_onchip(c, stk):
    nc = c.nc
    E = stk.enter_context

    def sb(name, shape, dt):
        c.sb[name] = E(nc.sbuf_tensor(name, shape, dt))

    def ps(name, shape, dt):
        c.ps[name] = E(nc.psum_tensor(name, shape, dt))

    sb("XT", [128, 8192], BF16)
    sb("BIG", [128, 32768], BF16)
    sb("MED", [128, 16384], BF16)
    sb("WA", [128, 8192], BF16)
    sb("WB", [128, 8192], BF16)
    sb("EA", [128, 2048], F32)
    sb("EB", [128, 2048], F32)
    sb("QSB", [128, 512], BF16)
    sb("SGB", [128, 512], BF16)
    sb("ST1", [128, 8], F32)
    sb("ST2", [128, 8], F32)
    sb("RR", [128, 4], F32)
    sb("SCQ", [128, 2048], F32)
    sb("IDF", [128, 128], F32)
    sb("IDB", [128, 128], BF16)
    sb("CMASK", [128, 128], F32)
    sb("CWT", [128, 128], F32)
    sb("LT", [128, 16], F32)
    sb("GW", [8, 2048], BF16)
    sb("GWT", [128, 256], BF16)
    sb("ROWA", [1, 2048], F32)
    sb("ROWB", [1, 2048], F32)
    sb("ROWC", [1, 512], F32)
    sb("ROWD", [1, 128], F32)
    sb("ROWE", [1, 128], F32)
    sb("NLB", [128, 128], BF16)
    sb("GB", [128, 128], F32)
    sb("BB", [128, 128], F32)
    sb("SCL", [128, 128], BF16)
    sb("RSC", [128, 256], F32)
    sb("CONSTF", [128, 8], F32)

    ps("PSB", [128, 2048], F32)
    ps("PF0", [128, 512], F32)
    ps("PF1", [128, 512], F32)
    ps("PF2", [128, 512], F32)
    ps("PF3", [128, 512], F32)



def ph_conv(c, sm):
    g = sm["GLD"]
    for a, b in [("wq", "wq_bf"), ("wk", "wk_bf"), ("wv", "wv_bf"),
                 ("wo", "wo_bf")]:
        c.dma("gpsimd", c.t[b], c.t[a], sem=g)


def ph_conv_rest(c, sm):
    g = sm["GLD2"]
    for a, b in [("wqkv", "wqkv_bf"), ("wo1", "wo1_bf"), ("sw1", "sw1_bf"),
                 ("sw2", "sw2_bf"), ("sw3", "sw3_bf")]:
        c.dma("gpsimd", c.t[b], c.t[a], sem=g)
    for i in range(8):
        c.dma("gpsimd", c.t["w1_bf"][i * D:(i + 1) * D, :],
              c.t["w1"][i * D:(i + 1) * D, :], sem=g)
        c.dma("gpsimd", c.t["w2_bf"][i * DI:(i + 1) * DI, :],
              c.t["w2"][i * DI:(i + 1) * DI, :], sem=g)


def _quant_one(c, sm, src, dst, nrows, W):
    G = W // 64
    BIG, MED = c.sb["BIG"], c.sb["MED"]
    SCL, RSC = c.sb["SCL"], c.sb["RSC"]
    GLD, VE, SC, ST = sm["GLD"], sm["VE"], sm["SC"], sm["ST"]
    MEDf = MED[:, :].bitcast(F32)
    for r0 in range(0, nrows, 128):
        qw = BIG[:, 0:W]
        qb = BIG[:, 8192:8192 + W]
        dd = BIG[:, 16384:16384 + W]
        med = MEDf[:, 0:W]
        c.dma("gpsimd", qw, src[r0:r0 + 128, :], sem=GLD)
        c.wait("scalar", GLD)
        c.op("scalar", "activation", med, qw, AF.Abs, sem=SC)
        c.wait("vector", SC)
        m3 = med.rearrange("p (g s) -> p g s", s=64)
        c.op("vector", "reduce_sum", RSC[:, 0:G], m3, axis=AX.X)
        c.op("vector", "tensor_scalar", SCL[:, 0:G], RSC[:, 0:G],
             1.0 / 64.0, BF16_1EM8, op0=OP.mult, op1=OP.max, sem=VE)
        c.wait("scalar", VE)
        c.op("scalar", "activation", RSC[:, 128:128 + G], SCL[:, 0:G], AF.Ln,
             sem=SC)
        c.wait("scalar", SC)
        c.op("scalar", "activation", RSC[:, 0:G], RSC[:, 128:128 + G], AF.Exp,
             scale=-1.0, sem=SC)
        c.wait("vector", SC)
        w3 = qw.rearrange("p (g s) -> p g s", s=64)
        q3 = qb.rearrange("p (g s) -> p g s", s=64)
        d3 = dd.rearrange("p (g s) -> p g s", s=64)
        med3 = med.rearrange("p (g s) -> p g s", s=64)
        rsb = RSC[:, 0:G].unsqueeze(2).broadcast_to([128, G, 64])
        scb = SCL[:, 0:G].unsqueeze(2).broadcast_to([128, G, 64])
        c.op("vector", "tensor_tensor", med3, w3, rsb, op=OP.mult)
        c.op("vector", "tensor_copy", qb, med)
        c.op("vector", "tensor_scalar", med, qb, MAGIC, MAGIC,
             op0=OP.add, op1=OP.subtract)
        c.op("vector", "tensor_scalar", med, med, 1.0, -1.0,
             op0=OP.min, op1=OP.max)
        c.op("vector", "tensor_tensor", q3, med3, scb, op=OP.mult)
        c.op("vector", "tensor_tensor", dd, qb, qw, op=OP.subtract)
        c.op("vector", "tensor_tensor", qb, qw, dd, op=OP.add, sem=VE)
        c.wait("sync", VE)
        c.dma("sync", dst[r0:r0 + 128, :], qb, sem=ST)
        c.wait("gpsimd", ST)
        c.wait("scalar", ST)


def ph_quant(c, sm):
    _quant_one(c, sm, c.t["wgu"], c.t["wgu_hat"], 2 * HF, D)
    _quant_one(c, sm, c.t["wp"], c.t["wp_hat"], D, HF)


def ph_ln(c, sm, src_name, dst_name):
    EA, EB, BIG = c.sb["EA"], c.sb["EB"], c.sb["BIG"]
    ST1 = c.sb["ST1"]
    LD, VE, SC, ST = sm["LD"], sm["VE"], sm["SC"], sm["ST"]
    src, dst = c.t[src_name], c.t[dst_name]
    MEDf = c.sb["MED"][:, :].bitcast(F32)
    ve_after_reduce = 0
    for tt in range(16):
        xa = EA[:, :]
        c.wait("sync", VE)
        c.wait("sync", SC)
        c.dma("sync", xa, src[tt * 128:(tt + 1) * 128, :], sem=LD)
        c.wait("vector", LD)
        c.wait("scalar", LD)
        c.op("vector", "reduce_sum", ST1[:, 0:1], xa, axis=AX.X)
        c.wait("scalar", VE, ve_after_reduce)
        c.op("scalar", "activation", MEDf[:, 0:2048], xa, AF.Square, sem=SC)
        c.wait("vector", SC)
        c.op("vector", "reduce_sum", ST1[:, 1:2], MEDf[:, 0:2048], axis=AX.X,
             sem=VE)
        ve_after_reduce = VE.n
        c.op("vector", "tensor_scalar", ST1[:, 2:3], ST1[:, 0:1], 1.0 / D, None,
             op0=OP.mult)
        c.op("vector", "tensor_scalar", ST1[:, 3:4], ST1[:, 2:3], ST1[:, 2:3],
             None, op0=OP.mult)
        c.op("vector", "tensor_scalar", ST1[:, 4:5], ST1[:, 1:2], 1.0 / D,
             ST1[:, 3:4], op0=OP.mult, op1=OP.subtract, sem=VE)
        c.wait("scalar", VE)
        c.op("scalar", "activation", ST1[:, 5:6], ST1[:, 4:5], AF.Ln,
             bias=1e-5, sem=SC)
        c.wait("scalar", SC)
        c.op("scalar", "activation", ST1[:, 6:7], ST1[:, 5:6], AF.Exp,
             scale=-0.5, sem=SC)
        c.wait("vector", SC)
        xnb = BIG[:, (tt % 2) * 2048:(tt % 2) * 2048 + 2048]
        c.wait("vector", ST, max(0, sm["ST"].n - 16))
        c.op("vector", "tensor_scalar", xnb, xa, ST1[:, 2:3], ST1[:, 6:7],
             op0=OP.subtract, op1=OP.mult, sem=VE)
        c.wait("sync", VE)
        c.dma("sync", dst[tt * 128:(tt + 1) * 128, :], xnb, sem=ST)


def _load_xnT(c, sm, src, tg):
    XT = c.sb["XT"]
    LD = sm["LD"]
    for dc in range(16):
        c.dma("sync", XT[:, dc * 512:(dc + 1) * 512],
              src[tg * 512:(tg + 1) * 512, dc * 128:(dc + 1) * 128],
              sem=LD, transpose=True)


def ph_l0qkv(c, sm):
    XT, WA, BIG, EA, QSB, SCQ = (c.sb["XT"], c.sb["WA"], c.sb["BIG"],
                                 c.sb["EA"], c.sb["QSB"], c.sb["SCQ"])
    ROWA, ROWC = c.sb["ROWA"], c.sb["ROWC"]
    LD, MM, VE, SC, ST = sm["LD"], sm["MM"], sm["VE"], sm["SC"], sm["ST"]
    PF = [c.ps["PF0"], c.ps["PF1"], c.ps["PF2"], c.ps["PF3"]]
    xn = c.t["xn_tok"]
    c.op("vector", "memset", BIG[:, 0:2048], 0, sem=VE)
    c.wait("sync", VE)
    c.dma("sync", c.t["k0_tok"][KPAD - 51:KPAD, :], BIG[0:51, 0:2048], sem=ST)
    c.dma("sync", c.t["v0_tok"][KPAD - 51:KPAD, :], BIG[0:51, 0:2048], sem=ST)
    c.dma("sync", ROWC[:, :], c.t["c_ones"], sem=LD)

    kv_in_tg = {0: [0, 1], 3: [0, 1, 2, 3]}

    for tg in range(4):
        c.wait("sync", MM)
        _load_xnT(c, sm, xn, tg)
        c.dma("sync", SCQ[:, :].rearrange("p (q x) -> p q x", q=4),
              c.t["c_scq"][tg * 512:(tg + 1) * 512, :]
              .rearrange("(q p) x -> p q x", p=128), sem=LD)
        parts = [("wq_bf", "bq", "q"), ("wk_bf", "bk", "k"), ("wv_bf", "bv", "v")]
        for wname, bname, which in parts:
            if which != "q" and tg not in kv_in_tg:
                continue
            c.wait("sync", MM)
            c.dma("sync", ROWA[:, :], c.t[bname], sem=LD)
            for es in range(4):
                wbuf = c.sb["WA"] if es % 2 == 0 else c.sb["WB"]
                c.wait("sync", MM, c.lu.get(("qkvw", wbuf is c.sb["WB"])))
                for dc in range(16):
                    c.dma("sync", wbuf[:, dc * 512:(dc + 1) * 512],
                          c.t[wname][es * 512:(es + 1) * 512,
                                     dc * 128:(dc + 1) * 128],
                          sem=LD, transpose=True)
                ld_here = LD.n
                qts = range(4) if which == "q" else kv_in_tg[tg]
                for qt in qts:
                    pf = PF[qt % 4]
                    c.wait("tensor", LD, ld_here)
                    c.wait("tensor", SC)
                    c.wait("tensor", VE)
                    for dc in range(16):
                        c.mm(pf[:, :], XT[:, dc * 512 + qt * 128:dc * 512 + qt * 128 + 128],
                             wbuf[:, dc * 512:(dc + 1) * 512],
                             start=(dc == 0), stop=False)
                    c.mm(pf[:, :], ROWC[0:1, 0:128],
                         ROWA[0:1, es * 512:(es + 1) * 512],
                         start=False, stop=True, sem=MM, skip_group_check=True)
                    c.lu[("qkvw", wbuf is c.sb["WB"])] = MM.n
                    tok0 = tg * 512 + qt * 128
                    if which == "v":
                        c.wait("scalar", MM)
                        c.op("scalar", "activation", QSB[:, :], pf[:, :],
                             AF.Copy, sem=SC)
                        c.wait("sync", SC)
                        ti = [i for i, (t0, n, r0) in enumerate(KSEL_TILES)
                              if t0 == tok0][0]
                        t0_, n_, r0_ = KSEL_TILES[ti]
                        c.dma("sync", c.t["v0_tok"][r0_:r0_ + n_,
                                                    es * 512:(es + 1) * 512],
                              QSB[0:n_, :], sem=ST)
                    else:
                        c.wait("vector", MM)
                        sc_s = SCQ[:, qt * 512:qt * 512 + 256] \
                            .rearrange("p (h d) -> p h d", h=4)
                        sc_c = SCQ[:, qt * 512 + 256:qt * 512 + 512] \
                            .rearrange("p (h d) -> p h d", h=4)
                        pev = pf[:, :].rearrange("p (h d j) -> p h d j", d=64, j=2)[:, :, :, 0]
                        pod = pf[:, :].rearrange("p (h d j) -> p h d j", d=64, j=2)[:, :, :, 1]
                        t1 = EA[:, 0:256].rearrange("p (h d) -> p h d", h=4)
                        t2 = EA[:, 256:512].rearrange("p (h d) -> p h d", h=4)
                        t3 = EA[:, 512:768].rearrange("p (h d) -> p h d", h=4)
                        t4 = EA[:, 768:1024].rearrange("p (h d) -> p h d", h=4)
                        ob = QSB[:, :].rearrange("p (h d) -> p h d", h=4, d=128)
                        o1 = ob[:, :, 0:64]
                        o2 = ob[:, :, 64:128]
                        c.op("vector", "tensor_tensor", t1, pev, sc_c, op=OP.mult)
                        c.op("vector", "tensor_tensor", t2, pod, sc_s, op=OP.mult)
                        c.op("vector", "tensor_tensor", t3, pev, sc_s, op=OP.mult)
                        c.op("vector", "tensor_tensor", t4, pod, sc_c, op=OP.mult)
                        c.op("vector", "tensor_tensor", o1, t1, t2, op=OP.subtract)
                        c.op("vector", "tensor_tensor", o2, t3, t4, op=OP.add, sem=VE)
                        c.wait("sync", VE)
                        if which == "q":
                            c.dma("sync", c.t["q0_tok"][tok0:tok0 + 128,
                                                        es * 512:(es + 1) * 512],
                                  QSB[:, :], sem=ST)
                        else:
                            ti = [i for i, (t0, n, r0) in enumerate(KSEL_TILES)
                                  if t0 == tok0][0]
                            t0_, n_, r0_ = KSEL_TILES[ti]
                            c.dma("sync", c.t["k0_tok"][r0_:r0_ + n_,
                                                        es * 512:(es + 1) * 512],
                                  QSB[0:n_, :], sem=ST)
                    c.wait("vector", ST)
                    c.wait("scalar", ST)


def ph_l0attn(c, sm):
    BIG, MED, EA = c.sb["BIG"], c.sb["MED"], c.sb["EA"]
    ST1 = c.sb["ST1"]
    IDB = c.sb["IDB"]
    LD, MM, VE, SC, ST = sm["LD"], sm["MM"], sm["VE"], sm["SC"], sm["ST"]
    PSB, PF0, PF1 = c.ps["PSB"], c.ps["PF0"], c.ps["PF1"]
    kT = BIG[:, 0:768]
    vh = BIG[:, 1024:1024 + 768]
    qT = BIG[:, 2048:4096]
    abf = BIG[:, 4096:4864]
    atb = BIG[:, 4864:5632]
    c.dma("sync", IDB[:, :], c.t["c_idb"], sem=LD)
    for h in range(16):
        c.wait("sync", MM)
        c.dma("sync", kT, c.t["k0_tok"][0:768, h * 128:(h + 1) * 128],
              sem=LD, transpose=True)
        c.dma("sync", vh.rearrange("p (k d) -> p k d", k=6),
              c.t["v0_tok"][:, h * 128:(h + 1) * 128]
              .rearrange("(k p) d -> p k d", p=128), sem=LD)
        c.dma("sync", qT, c.t["q0_tok"][:, h * 128:(h + 1) * 128],
              sem=LD, transpose=True)
        for qt in range(16):
            c.wait("tensor", LD)
            c.wait("tensor", SC)
            c.mm(PSB[:, 0:512], qT[:, qt * 128:(qt + 1) * 128], kT[:, 0:512],
                 start=True, stop=True)
            c.mm(PSB[:, 512:768], qT[:, qt * 128:(qt + 1) * 128], kT[:, 512:768],
                 start=True, stop=True, sem=MM)
            c.wait("vector", MM)
            c.op("vector", "memset", PSB[:, 717:768], NEG)
            c.op("vector", "reduce_max", ST1[:, 0:1], PSB[:, 0:768], axis=AX.X,
                 negate=True)
            c.op("vector", "tensor_scalar", ST1[:, 1:2], ST1[:, 0:1], SC0, None,
                 op0=OP.mult, sem=VE)
            c.wait("scalar", VE)
            c.wait("scalar", MM)
            c.op("scalar", "activation", abf, PSB[:, 0:768], AF.Exp,
                 bias=ST1[:, 1:2], scale=SC0, accum_out=ST1[:, 2:3])
            c.op("scalar", "activation", ST1[:, 4:5], ST1[:, 2:3], AF.Ln,
                 sem=SC)
            c.wait("scalar", SC)
            c.op("scalar", "activation", ST1[:, 3:4], ST1[:, 4:5], AF.Exp,
                 scale=-1.0, sem=SC)
            c.wait("vector", SC)
            c.op("vector", "tensor_scalar", abf, abf, ST1[:, 3:4], None,
                 op0=OP.mult, sem=VE)
            c.wait("tensor", VE)
            ptb = PF0[:, :].bitcast(BF16)
            for k in range(6):
                c.op("tensor", "transpose", ptb[:, k * 128:(k + 1) * 128],
                     abf[:, k * 128:(k + 1) * 128], IDB[:, :],
                     sem=(MM if k == 5 else None))
            c.wait("vector", MM)
            c.op("vector", "tensor_copy", atb[:, 0:768], ptb[:, 0:768], sem=VE)
            c.wait("tensor", VE)
            for k in range(6):
                c.mm(PF1[:, 0:128], vh[:, k * 128:(k + 1) * 128],
                     atb[:, k * 128:(k + 1) * 128],
                     start=(k == 0), stop=(k == 5), sem=(MM if k == 5 else None))
            c.wait("scalar", MM)
            c.op("scalar", "activation", MED[:, qt * 128:(qt + 1) * 128],
                 PF1[:, 0:128], AF.Copy, sem=SC)
        c.wait("sync", SC)
        c.dma("sync", c.t["oT0"][h * 128:(h + 1) * 128, :], MED[:, 0:2048],
              sem=ST)
        c.wait("scalar", ST, sm["ST"].n)


def ph_wo(c, sm, oT_name, w_name, bias_name, dst_name):
    WA, BIG, EA = c.sb["WA"], c.sb["BIG"], c.sb["EA"]
    ROWA, ROWC = c.sb["ROWA"], c.sb["ROWC"]
    LD, MM, VE, ST = sm["LD"], sm["MM"], sm["VE"], sm["ST"]
    PF = [c.ps["PF0"], c.ps["PF1"], c.ps["PF2"], c.ps["PF3"]]
    oT, wbf, dst = c.t[oT_name], c.t[w_name], c.t[dst_name]
    c.wait("sync", sm["GLD2"])
    if bias_name:
        c.dma("sync", ROWA[:, :], c.t[bias_name], sem=LD)
    c.dma("sync", ROWC[:, :], c.t["c_ones"], sem=LD)
    wo_bufs = (WA, c.sb["WB"])
    for ds in range(4):
        wbuf = wo_bufs[ds % 2]
        c.wait("sync", MM, c.lu.get(("wow", ds % 2)))
        for dc in range(16):
            c.dma("sync", wbuf[:, dc * 512:(dc + 1) * 512],
                  wbf[ds * 512:(ds + 1) * 512, dc * 128:(dc + 1) * 128],
                  sem=LD, transpose=True)
        ld_w = LD.n
        for qt in range(16):
            otb = BIG[:, (qt % 4) * 2048:(qt % 4) * 2048 + 2048]
            c.wait("sync", MM, c.lu.get(("wot", qt % 4)))
            c.dma("sync", otb.rearrange("p (k t) -> p k t", k=16),
                  oT.rearrange("(k p) t -> p k t", p=128)
                  [:, :, qt * 128:(qt + 1) * 128], sem=LD)
            ld_here = LD.n
            pf = PF[qt % 4]
            c.wait("tensor", LD, ld_here)
            c.wait("tensor", VE, c.lu.get(("wop", qt % 4)))
            for dc in range(16):
                c.mm(pf[:, :], otb[:, dc * 128:(dc + 1) * 128],
                     wbuf[:, dc * 512:(dc + 1) * 512],
                     start=(dc == 0), stop=(not bias_name and dc == 15),
                     sem=(MM if (not bias_name and dc == 15) else None))
            if bias_name:
                c.mm(pf[:, :], ROWC[0:1, 0:128],
                     ROWA[0:1, ds * 512:(ds + 1) * 512],
                     start=False, stop=True, sem=MM, skip_group_check=True)
            c.lu[("wot", qt % 4)] = MM.n
            c.lu[("wow", ds % 2)] = MM.n
            c.wait("vector", MM)
            c.wait("vector", ST, max(0, sm["ST"].n - 3 * 16))
            ebuf = EA[:, (qt % 4) * 512:(qt % 4) * 512 + 512]
            c.op("vector", "tensor_copy", ebuf, pf[:, :], sem=VE)
            c.lu[("wop", qt % 4)] = VE.n
            c.wait("sync", VE)
            c.dma("sync", dst[qt * 128:(qt + 1) * 128, ds * 512:(ds + 1) * 512],
                  ebuf, sem=ST)


def ph_l0ffn(c, sm):
    XT, WA, WB, BIG, EA, SGB = (c.sb["XT"], c.sb["WA"], c.sb["WB"],
                                c.sb["BIG"], c.sb["EA"], c.sb["SGB"])
    IDF, RR, ST1 = c.sb["IDF"], c.sb["RR"], c.sb["ST1"]
    LD, MM, VE, SC, ST = sm["LD"], sm["MM"], sm["VE"], sm["SC"], sm["ST"]
    PSB = c.ps["PSB"]
    PF0, PF1, PF2 = c.ps["PF0"], c.ps["PF1"], c.ps["PF2"]
    wgu, wp = c.t["wgu_hat"], c.t["wp_hat"]
    hT = BIG[:, 0:32768]
    c.dma("sync", IDF[:, :], c.t["c_idf"], sem=LD)
    for sp in range(4):
        c.wait("sync", MM)
        _load_xnT(c, sm, c.t["xn_tok"], sp)
        MEDb = c.sb["MED"]
        c.lu[("ffng", 0)] = c.lu[("ffng", 1)] = MM.n
        for hg in range(16):
            gbuf = WA if hg % 2 == 0 else WB
            ubuf = MEDb[:, 0:8192] if hg % 2 == 0 else MEDb[:, 8192:16384]
            c.wait("sync", MM, c.lu.get(("ffng", hg % 2)))
            for dc in range(16):
                c.dma("sync", gbuf[:, dc * 512:(dc + 1) * 512],
                      wgu[hg * 512:(hg + 1) * 512, dc * 128:(dc + 1) * 128],
                      sem=LD, transpose=True)
                c.dma("sync", ubuf[:, dc * 512:(dc + 1) * 512],
                      wgu[HF + hg * 512:HF + (hg + 1) * 512,
                          dc * 128:(dc + 1) * 128],
                      sem=LD, transpose=True)
            ld_here = LD.n
            c.wait("tensor", LD, ld_here)
            for j in range(4):
                hb = hg * 4 + j
                c.wait("tensor", SC)
                for dc in range(16):
                    c.mm(PF0[:, :], gbuf[:, dc * 512 + j * 128:dc * 512 + j * 128 + 128],
                         XT[:, dc * 512:(dc + 1) * 512],
                         start=(dc == 0), stop=(dc == 15),
                         sem=(MM if dc == 15 else None))
                c.wait("tensor", VE)
                for dc in range(16):
                    c.mm(PF1[:, :], ubuf[:, dc * 512 + j * 128:dc * 512 + j * 128 + 128],
                         XT[:, dc * 512:(dc + 1) * 512],
                         start=(dc == 0), stop=(dc == 15),
                         sem=(MM if dc == 15 else None))
                if j == 3:
                    c.lu[("ffng", hg % 2)] = MM.n
                c.wait("scalar", MM, MM.n - 1)
                c.op("scalar", "activation", SGB[:, :], PF0[:, :], AF.Silu,
                     sem=SC)
                c.wait("vector", SC)
                c.wait("vector", MM)
                c.op("vector", "tensor_tensor", hT[:, hb * 512:(hb + 1) * 512],
                     SGB[:, :], PF1[:, :], op=OP.mult, sem=VE)
                c.wait("tensor", VE)
                for st in range(4):
                    c.mm(PF2[:, st * 128:(st + 1) * 128],
                         hT[:, hb * 512 + st * 128:hb * 512 + st * 128 + 128],
                         hT[:, hb * 512 + st * 128:hb * 512 + st * 128 + 128],
                         start=(hb == 0), stop=(hb == 63),
                         sem=(MM if hb == 63 and st == 3 else None))
        c.wait("vector", MM)
        for st in range(4):
            c.op("vector", "tensor_tensor", EA[:, 0:128],
                 PF2[:, st * 128:(st + 1) * 128], IDF[:, :], op=OP.mult)
            c.op("vector", "reduce_sum", ST1[:, 0:1], EA[:, 0:128], axis=AX.X)
            c.op("vector", "tensor_scalar", ST1[:, 1:2], ST1[:, 0:1],
                 1.0 / HF, RMS_EPS, op0=OP.mult, op1=OP.add, sem=VE)
            c.wait("scalar", VE)
            c.op("scalar", "activation", ST1[:, 2:3], ST1[:, 1:2], AF.Ln,
                 sem=SC)
            c.wait("scalar", SC)
            c.op("scalar", "activation", RR[:, st:st + 1], ST1[:, 2:3], AF.Exp,
                 scale=-0.5, sem=SC)
            c.wait("vector", SC)
        gu_done = MM.n
        for ds in range(4):
            for hb in range(64):
                slot = hb % 32
                sbuf_ = WA if slot < 16 else WB
                wslot = sbuf_[:, (slot % 16) * 512:(slot % 16) * 512 + 512]
                c.wait("sync", MM, max(c.lu.get(("ffny", slot), 0), gu_done))
                c.dma("sync", wslot,
                      wp[ds * 512:(ds + 1) * 512, hb * 128:(hb + 1) * 128],
                      sem=LD, transpose=True)
                ld_here = LD.n
                c.wait("tensor", LD, ld_here)
                if hb == 0:
                    c.wait("tensor", SC)
                for st in range(4):
                    c.mm(PSB[:, st * 512:(st + 1) * 512],
                         hT[:, hb * 512 + st * 128:hb * 512 + st * 128 + 128],
                         wslot, start=(hb == 0), stop=(hb == 63),
                         sem=(MM if st == 3 else None))
                c.lu[("ffny", slot)] = MM.n
            c.wait("scalar", MM)
            for st in range(4):
                tok0 = sp * 512 + st * 128
                c.wait("scalar", VE)
                c.op("scalar", "activation", EA[:, 0:512],
                     PSB[:, st * 512:(st + 1) * 512], AF.Silu,
                     scale=RR[:, st:st + 1], sem=SC)
                c.wait("gpsimd", VE)
                c.dma("gpsimd", EA[:, 512:1024],
                      c.t["x"][tok0:tok0 + 128, ds * 512:(ds + 1) * 512],
                      sem=sm["GLD"])
                c.dma("gpsimd", EA[:, 1024:1536],
                      c.t["s0"][tok0:tok0 + 128, ds * 512:(ds + 1) * 512],
                      sem=sm["GLD"])
                c.wait("vector", SC)
                c.wait("vector", sm["GLD"])
                c.wait("vector", ST)
                c.op("vector", "tensor_tensor", EA[:, 1536:2048], EA[:, 0:512],
                     EA[:, 1024:1536], op=OP.mult)
                c.op("vector", "tensor_tensor", EA[:, 1024:1536],
                     EA[:, 1536:2048], EA[:, 512:1024], op=OP.add, sem=VE)
                c.wait("sync", VE)
                c.dma("sync", c.t["x1"][tok0:tok0 + 128, ds * 512:(ds + 1) * 512],
                      EA[:, 1024:1536], sem=ST)
                c.wait("scalar", ST)


L1QKV_LOGITS = True
L1QKV_PROJ = True
L1QKV_PARTS = {"mm", "copy", "store"}
DBG_CW = False


def ph_l1qkv(c, sm):
    XT, WA, QSB, EA = c.sb["XT"], c.sb["WA"], c.sb["QSB"], c.sb["EA"]
    GW, GWT = c.sb["GW"], c.sb["GWT"]
    LD, GLD, MM, VE, SC, ST = (sm["LD"], sm["GLD"], sm["MM"], sm["VE"],
                               sm["SC"], sm["ST"])
    PF = [c.ps["PF0"], c.ps["PF1"], c.ps["PF2"]]
    PF3 = c.ps["PF3"]
    IDB = c.sb["IDB"]
    wqkv = c.t["wqkv_bf"]
    outs = {0: "q1_tok", 1: "k1_tok", 2: "v1_tok"}
    c.wait("sync", sm["GLD2"])
    c.wait("gpsimd", sm["GLD2"])
    c.op("vector", "memset", c.sb["BIG"][0:8, 0:2048], 0, sem=VE)
    c.dma("gpsimd", c.t["gate_pad"][0:8, :], c.t["gatew"], sem=GLD)
    c.wait("sync", VE)
    c.wait("sync", GLD)
    c.dma("sync", c.t["gate_pad"][8:16, :], c.sb["BIG"][0:8, 0:2048], sem=ST)
    c.wait("sync", ST)
    for dc in range(16):
        c.dma("sync", GWT[:, dc * 16:(dc + 1) * 16],
              c.t["gate_pad"][:, dc * 128:(dc + 1) * 128],
              sem=LD, transpose=True)
    for tg in range(4):
        c.wait("sync", MM)
        _load_xnT(c, sm, c.t["xn1_tok"], tg)
        c.wait("tensor", LD)
        c.wait("tensor", VE)
        for qt in (range(4) if L1QKV_LOGITS else []):
            if "mm" in L1QKV_PARTS:
                c.wait("tensor", VE)
                for dc in range(16):
                    c.mm(PF3[:, 0:8],
                         XT[:, dc * 512 + qt * 128:dc * 512 + qt * 128 + 128],
                         GWT[:, dc * 16:dc * 16 + 8],
                         start=(dc == 0), stop=(dc == 15),
                         sem=(MM if dc == 15 else None))
            if "copy" in L1QKV_PARTS:
                c.wait("vector", MM)
                c.op("vector", "tensor_copy", EA[:, qt * 8:qt * 8 + 8],
                     PF3[:, 0:8], sem=VE)
        c.wait("sync", VE)
        for qt in (range(4) if (L1QKV_LOGITS and "store" in L1QKV_PARTS)
                   else []):
            c.dma("sync", c.t["logits"][(tg * 4 + qt) * 128:(tg * 4 + qt + 1) * 128, :],
                  EA[:, qt * 8:qt * 8 + 8], sem=ST,
                  allow_slow_non_contiguous=True)
        for es in (range(12) if L1QKV_PROJ else []):
            wbuf = WA if es % 2 == 0 else c.sb["WB"]
            c.wait("sync", MM, c.lu.get(("l1qw", es % 2)))
            for dc in range(16):
                c.dma("sync", wbuf[:, dc * 512:(dc + 1) * 512],
                      wqkv[es * 512:(es + 1) * 512, dc * 128:(dc + 1) * 128],
                      sem=LD, transpose=True)
            ld_here = LD.n
            for qt in range(4):
                pf = PF[qt % 3]
                c.wait("tensor", LD, ld_here)
                c.wait("tensor", SC, c.lu.get(("l1qp", qt % 3)))
                for dc in range(16):
                    c.mm(pf[:, :],
                         XT[:, dc * 512 + qt * 128:dc * 512 + qt * 128 + 128],
                         wbuf[:, dc * 512:(dc + 1) * 512],
                         start=(dc == 0), stop=(dc == 15),
                         sem=(MM if dc == 15 else None))
                c.lu[("l1qw", es % 2)] = MM.n
                qsb = QSB if qt % 2 == 0 else c.sb["SGB"]
                c.wait("scalar", MM)
                c.wait("scalar", ST, c.lu.get(("l1qs", qt % 2)))
                c.op("scalar", "activation", qsb[:, :], pf[:, :], AF.Copy,
                     sem=SC)
                c.lu[("l1qp", qt % 3)] = SC.n
                c.wait("sync", SC)
                tok0 = tg * 512 + qt * 128
                c.dma("sync", c.t[outs[es // 4]][tok0:tok0 + 128,
                                                 (es % 4) * 512:(es % 4) * 512 + 512],
                      qsb[:, :], sem=ST)
                c.lu[("l1qs", qt % 2)] = ST.n


def ph_l1attn(c, sm):
    BIG, MED, EA = c.sb["BIG"], c.sb["MED"], c.sb["EA"]
    ST1, ST2 = c.sb["ST1"], c.sb["ST2"]
    IDB, CMASK = c.sb["IDB"], c.sb["CMASK"]
    ROWC, ROWD, ROWE = c.sb["ROWC"], c.sb["ROWD"], c.sb["ROWE"]
    NLB, GB, BB = c.sb["NLB"], c.sb["GB"], c.sb["BB"]
    LD, MM, VE, SC, ST = sm["LD"], sm["MM"], sm["VE"], sm["SC"], sm["ST"]
    PSB = c.ps["PSB"]
    PF0, PF1, PF2, PF3 = c.ps["PF0"], c.ps["PF1"], c.ps["PF2"], c.ps["PF3"]
    qT = BIG[:, 0:2048]
    kT = BIG[:, 2048:4096]
    vh = BIG[:, 4096:6144]
    v2h = BIG[:, 6144:8192]
    a1 = BIG[:, 8192:10240]
    a2 = BIG[:, 10240:12288]
    at1 = BIG[:, 12288:14336]
    at2 = BIG[:, 14336:16384]
    otb = BIG[:, 16384:18432]

    c.dma("sync", IDB[:, :], c.t["c_idb"], sem=LD)
    c.dma("sync", CMASK[:, :], c.t["c_cmask"], sem=LD)
    c.dma("sync", ROWC[:, :], c.t["c_ones"], sem=LD)
    c.dma("sync", EA[0:1, 0:128], c.t["lq1"], sem=LD)
    c.dma("sync", EA[0:1, 128:256], c.t["lk1"], sem=LD)
    c.dma("sync", EA[0:1, 256:384], c.t["lq2"], sem=LD)
    c.dma("sync", EA[0:1, 384:512], c.t["lk2"], sem=LD)
    c.wait("vector", LD)
    c.op("vector", "tensor_tensor", EA[0:1, 512:640], EA[0:1, 0:128],
         EA[0:1, 128:256], op=OP.mult)
    c.op("vector", "tensor_tensor", EA[0:1, 640:768], EA[0:1, 256:384],
         EA[0:1, 384:512], op=OP.mult, sem=VE)
    c.wait("scalar", VE)
    c.op("scalar", "activation", EA[0:1, 768:896], EA[0:1, 512:640], AF.Exp)
    c.op("scalar", "activation", EA[0:1, 896:1024], EA[0:1, 640:768], AF.Exp,
         sem=SC)
    c.wait("vector", SC)
    c.op("vector", "tensor_tensor", ROWD[:, :], EA[0:1, 896:1024],
         EA[0:1, 768:896], op=OP.subtract)
    c.op("vector", "tensor_scalar", ROWD[:, :], ROWD[:, :], LAMBDA_INIT, None,
         op0=OP.subtract, sem=VE)
    c.wait("tensor", VE)
    c.mm(PF3[:, 0:128], ROWC[0:1, 0:128], ROWD[0:1, :], start=True, stop=True,
         sem=MM)
    c.wait("vector", MM)
    c.op("vector", "tensor_copy", NLB[:, :], PF3[:, 0:128], sem=VE)

    for h in range(16):
        c.wait("sync", VE)
        c.wait("sync", MM)
        c.dma("sync", ROWD[:, :], c.t["gng"][h:h + 1, :], sem=LD)
        c.dma("sync", ROWE[:, :], c.t["gnb"][h:h + 1, :], sem=LD)
        c.wait("vector", LD)
        c.op("vector", "tensor_scalar", ROWD[:, :], ROWD[:, :],
             1.0 - LAMBDA_INIT, None, op0=OP.mult)
        c.op("vector", "tensor_scalar", ROWE[:, :], ROWE[:, :],
             1.0 - LAMBDA_INIT, None, op0=OP.mult, sem=VE)
        c.wait("tensor", VE)
        c.wait("tensor", SC)
        c.mm(PF3[:, 0:128], ROWC[0:1, 0:128], ROWD[0:1, :], start=True,
             stop=True)
        c.mm(PF3[:, 128:256], ROWC[0:1, 0:128], ROWE[0:1, :], start=True,
             stop=True, sem=MM)
        c.wait("vector", MM)
        c.op("vector", "tensor_copy", GB[:, :], PF3[:, 0:128])
        c.op("vector", "tensor_copy", BB[:, :], PF3[:, 128:256], sem=VE)
        c.dma("sync", qT, c.t["q1_tok"][:, h * 128:(h + 1) * 128], sem=LD,
              transpose=True)
        c.dma("sync", kT, c.t["k1_tok"][:, h * 128:(h + 1) * 128], sem=LD,
              transpose=True)
        c.dma("sync", vh.rearrange("p (k d) -> p k d", k=16),
              c.t["v1_tok"][:, h * 128:(h + 1) * 128]
              .rearrange("(k p) d -> p k d", p=128), sem=LD)
        c.wait("vector", LD)
        c.op("vector", "tensor_tensor",
             v2h.rearrange("p (k d) -> p k d", k=16),
             vh.rearrange("p (k d) -> p k d", k=16),
             NLB[:, :].unsqueeze(1).broadcast_to([128, 16, 128]),
             op=OP.mult, sem=VE)
        for qt in range(16):
            nk = qt + 1
            kend = nk * 128
            for half, (qs, ks, ab) in enumerate(
                    [(qT[0:64, qt * 128:(qt + 1) * 128], kT, a1),
                     (qT[64:128, qt * 128:(qt + 1) * 128], kT, a2)]):
                c.wait("tensor", LD)
                c.wait("tensor", VE)
                c.wait("tensor", SC)
                nch = (kend + 511) // 512
                for ch in range(nch):
                    n = min(512, kend - ch * 512)
                    ksl = (kT[0:64, ch * 512:ch * 512 + n] if half == 0
                           else kT[64:128, ch * 512:ch * 512 + n])
                    c.mm(PSB[:, ch * 512:ch * 512 + n], qs, ksl,
                         start=True, stop=True,
                         sem=(MM if ch == nch - 1 else None))
                c.wait("vector", MM)
                c.op("vector", "tensor_tensor", PSB[:, kend - 128:kend],
                     PSB[:, kend - 128:kend], CMASK[:, :], op=OP.add)
                c.op("vector", "reduce_max", ST1[:, 0:1], PSB[:, 0:kend],
                     axis=AX.X, negate=True)
                c.op("vector", "tensor_scalar", ST1[:, 1:2], ST1[:, 0:1], SC1,
                     None, op0=OP.mult, sem=VE)
                c.wait("scalar", VE)
                c.wait("scalar", MM)
                c.op("scalar", "activation", ab[:, 0:kend], PSB[:, 0:kend],
                     AF.Exp, bias=ST1[:, 1:2], scale=SC1,
                     accum_out=ST1[:, 2:3])
                c.op("scalar", "activation", ST1[:, 4:5], ST1[:, 2:3],
                     AF.Ln, sem=SC)
                c.wait("scalar", SC)
                c.op("scalar", "activation", ST1[:, 3:4], ST1[:, 4:5], AF.Exp,
                     scale=-1.0, sem=SC)
                c.wait("vector", SC)
                c.op("vector", "tensor_scalar", ab[:, 0:kend], ab[:, 0:kend],
                     ST1[:, 3:4], None, op0=OP.mult, sem=VE)
            pt0 = PF0[:, :].bitcast(BF16)
            pt1 = PF1[:, :].bitcast(BF16)
            c.wait("tensor", VE)
            for k in range(nk):
                sl = (k % 8) * 128
                c.op("tensor", "transpose", pt0[:, sl:sl + 128],
                     a1[:, k * 128:(k + 1) * 128], IDB[:, :])
                c.op("tensor", "transpose", pt1[:, sl:sl + 128],
                     a2[:, k * 128:(k + 1) * 128], IDB[:, :],
                     sem=(MM if (k % 8 == 7 or k == nk - 1) else None))
                if k % 8 == 7 or k == nk - 1:
                    c.wait("vector", MM)
                    base = (k // 8) * 8 * 128
                    w = sl + 128 - (0 if True else 0)
                    c.op("vector", "tensor_copy",
                         at1[:, base:base + sl + 128], pt0[:, 0:sl + 128])
                    c.op("vector", "tensor_copy",
                         at2[:, base:base + sl + 128], pt1[:, 0:sl + 128],
                         sem=VE)
                    if k < nk - 1:
                        c.wait("tensor", VE)
            c.wait("tensor", VE)
            c.wait("tensor", SC)
            for k in range(nk):
                c.mm(PF2[:, 0:128], at1[:, k * 128:(k + 1) * 128],
                     vh[:, k * 128:(k + 1) * 128], start=(k == 0), stop=False)
            for k in range(nk):
                c.mm(PF2[:, 0:128], at2[:, k * 128:(k + 1) * 128],
                     v2h[:, k * 128:(k + 1) * 128], start=False,
                     stop=(k == nk - 1), sem=(MM if k == nk - 1 else None))
            c.wait("vector", MM)
            c.wait("scalar", MM)
            c.wait("scalar", VE)
            c.op("vector", "reduce_sum", ST2[:, 0:1], PF2[:, 0:128], axis=AX.X)
            c.op("scalar", "activation", EA[:, 0:128], PF2[:, 0:128],
                 AF.Square, sem=SC)
            c.op("vector", "tensor_scalar", ST2[:, 1:2], ST2[:, 0:1], 1.0 / 128,
                 None, op0=OP.mult)
            c.op("vector", "tensor_scalar", ST2[:, 2:3], ST2[:, 1:2],
                 ST2[:, 1:2], None, op0=OP.mult)
            c.wait("vector", SC)
            c.op("vector", "reduce_sum", ST2[:, 3:4], EA[:, 0:128], axis=AX.X)
            c.op("vector", "tensor_scalar", ST2[:, 4:5], ST2[:, 3:4],
                 1.0 / 128, ST2[:, 2:3], op0=OP.mult, op1=OP.subtract, sem=VE)
            c.wait("scalar", VE)
            c.op("scalar", "activation", ST2[:, 5:6], ST2[:, 4:5], AF.Ln,
                 bias=1e-5, sem=SC)
            c.wait("scalar", SC)
            c.op("scalar", "activation", ST2[:, 6:7], ST2[:, 5:6], AF.Exp,
                 scale=-0.5, sem=SC)
            c.wait("vector", SC)
            c.op("vector", "tensor_scalar", EA[:, 128:256], PF2[:, 0:128],
                 ST2[:, 1:2], ST2[:, 6:7], op0=OP.subtract, op1=OP.mult)
            c.op("vector", "tensor_tensor", EA[:, 256:384], EA[:, 128:256],
                 GB[:, :], op=OP.mult)
            c.op("vector", "tensor_tensor", c.sb["QSB"][:, 0:128],
                 EA[:, 256:384], BB[:, :], op=OP.add, sem=VE)
            c.wait("tensor", VE)
            pt0 = PF3[:, :].bitcast(BF16)
            c.op("tensor", "transpose", pt0[:, 0:128], c.sb["QSB"][:, 0:128],
                 IDB[:, :], sem=MM)
            c.wait("vector", MM)
            c.op("vector", "tensor_copy", otb[:, qt * 128:(qt + 1) * 128],
                 pt0[:, 0:128], sem=VE)
        c.wait("sync", VE)
        c.dma("sync", c.t["oT1"][h * 128:(h + 1) * 128, :], otb, sem=ST)
        c.wait("vector", ST)


def ph_l1moe(c, sm):
    XT, WA, WB, BIG, MED, EA, SGB = (c.sb["XT"], c.sb["WA"], c.sb["WB"],
                                     c.sb["BIG"], c.sb["MED"], c.sb["EA"],
                                     c.sb["SGB"])
    CWT, LT, ST1 = c.sb["CWT"], c.sb["LT"], c.sb["ST1"]
    LD, GLD, MM, VE, SC, ST = (sm["LD"], sm["GLD"], sm["MM"], sm["VE"],
                               sm["SC"], sm["ST"])
    PSB = c.ps["PSB"]
    PF0, PF1 = c.ps["PF0"], c.ps["PF1"]
    accf = MED[:, :].bitcast(F32)
    c.wait("sync", sm["GLD2"])
    c.wait("gpsimd", sm["GLD2"])
    hE = BIG[:, 0:5632]
    sg = BIG[:, 5632:11264]
    for tt in range(16):
        c.wait("sync", VE)
        c.dma("sync", LT[:, 0:8], c.t["logits"][tt * 128:(tt + 1) * 128, :],
              sem=LD, allow_slow_non_contiguous=True)
        c.wait("vector", LD)
        c.op("vector", "reduce_max", ST1[:, 0:1], LT[:, 0:8], axis=AX.X)
        c.op("vector", "tensor_scalar", LT[:, 8:16], LT[:, 0:8], ST1[:, 0:1],
             None, op0=OP.is_ge)
        c.op("vector", "scalar_tensor_tensor", EA[:, 0:8], LT[:, 8:16], NEG,
             LT[:, 0:8], op0=OP.mult, op1=OP.add)
        c.op("vector", "reduce_max", ST1[:, 1:2], EA[:, 0:8], axis=AX.X)
        c.op("vector", "tensor_scalar", EA[:, 8:16], EA[:, 0:8], ST1[:, 1:2],
             None, op0=OP.is_ge)
        c.op("vector", "tensor_scalar", ST1[:, 2:3], ST1[:, 1:2], ST1[:, 0:1],
             None, op0=OP.subtract, sem=VE)
        c.wait("scalar", VE)
        c.op("scalar", "activation", ST1[:, 5:6], ST1[:, 2:3], AF.Sigmoid,
             scale=-1.0)
        c.op("scalar", "activation", ST1[:, 6:7], ST1[:, 2:3], AF.Sigmoid,
             sem=SC)
        c.wait("vector", SC)
        c.op("vector", "tensor_scalar", EA[:, 16:24], LT[:, 8:16], ST1[:, 5:6],
             None, op0=OP.mult)
        c.op("vector", "scalar_tensor_tensor", CWT[:, tt * 8:(tt + 1) * 8],
             EA[:, 8:16], ST1[:, 6:7], EA[:, 16:24],
             op0=OP.mult, op1=OP.add, sem=VE)
        if DBG_CW:
            c.wait("sync", VE)
            c.dma("sync", c.t["dbg_cw"][tt * 128:(tt + 1) * 128, :],
                  CWT[:, tt * 8:(tt + 1) * 8], sem=ST,
                  allow_slow_non_contiguous=True)
    for sp in range(4):
        c.wait("sync", MM)
        _load_xnT(c, sm, c.t["xn1_tok"], sp)
        c.wait("vector", SC)
        for e in range(NE):
            for ib in range(22):
                slot = ib % 4
                wc = WA[:, slot * 2048:(slot + 1) * 2048]
                c.wait("sync", MM, c.lu.get(("moew1", slot)))
                c.dma("sync", wc.rearrange("p (k i) -> p k i", k=16),
                      c.t["w1_bf"][e * D:(e + 1) * D, ib * 128:(ib + 1) * 128]
                      .rearrange("(k p) i -> p k i", p=128), sem=LD)
                ld_here = LD.n
                pf = PF0 if ib % 2 == 0 else PF1
                c.wait("tensor", LD, ld_here)
                if ib >= 2:
                    c.wait("tensor", SC if ib % 2 == 0 else VE)
                for hc in range(16):
                    c.mm(pf[:, :], wc[:, hc * 128:(hc + 1) * 128],
                         XT[:, hc * 512:(hc + 1) * 512],
                         start=(hc == 0), stop=(hc == 15),
                         sem=(MM if hc == 15 else None))
                c.lu[("moew1", slot)] = MM.n
                if ib < 11:
                    c.wait("scalar", MM)
                    c.op("scalar", "activation", sg[:, ib * 512:(ib + 1) * 512],
                         pf[:, :], AF.Silu, sem=SC)
                else:
                    c.wait("vector", MM)
                    c.wait("vector", SC)
                    c.op("vector", "tensor_tensor",
                         hE[:, (ib - 11) * 512:(ib - 10) * 512],
                         sg[:, (ib - 11) * 512:(ib - 10) * 512], pf[:, :],
                         op=OP.mult, sem=VE)
            for ds in range(4):
                for ic in range(11):
                    slot = (ds * 11 + ic) % 8
                    wslot = WB[:, slot * 512:slot * 512 + 512]
                    c.wait("sync", MM, c.lu.get(("moew2", slot)))
                    c.dma("sync", wslot,
                          c.t["w2_bf"][e * DI + ic * 128:e * DI + (ic + 1) * 128,
                                       ds * 512:(ds + 1) * 512], sem=LD)
                    ld_here = LD.n
                    c.wait("tensor", LD, ld_here)
                    if ic == 0:
                        c.wait("tensor", VE)
                    for st in range(4):
                        c.mm(PSB[:, st * 512:(st + 1) * 512],
                             hE[:, ic * 512 + st * 128:ic * 512 + st * 128 + 128],
                             wslot, start=(ic == 0), stop=(ic == 10),
                             sem=(MM if st == 3 else None))
                    c.lu[("moew2", slot)] = MM.n
                c.wait("vector", MM)
                for st in range(4):
                    acc = accf[:, st * 2048 + ds * 512:st * 2048 + ds * 512 + 512]
                    cwcol = CWT[:, (sp * 4 + st) * 8 + e:(sp * 4 + st) * 8 + e + 1]
                    if e == 0:
                        c.op("vector", "tensor_scalar", acc,
                             PSB[:, st * 512:(st + 1) * 512], cwcol, None,
                             op0=OP.mult, sem=(VE if st == 3 else None))
                    else:
                        c.op("vector", "scalar_tensor_tensor", acc,
                             PSB[:, st * 512:(st + 1) * 512], cwcol, acc,
                             op0=OP.mult, op1=OP.add,
                             sem=(VE if st == 3 else None))
        c.lu[("moesh", 0)] = c.lu[("moesh", 1)] = MM.n
        for ib in range(11):
            base = 0 if ib % 2 == 0 else 4096
            c.wait("sync", MM, c.lu.get(("moesh", ib % 2)))
            for dc in range(16):
                c.dma("sync", WA[:, base + dc * 128:base + (dc + 1) * 128],
                      c.t["sw1_bf"][ib * 128:(ib + 1) * 128,
                                    dc * 128:(dc + 1) * 128],
                      sem=LD, transpose=True)
                c.dma("sync", WA[:, base + 2048 + dc * 128:base + 2048 + (dc + 1) * 128],
                      c.t["sw2_bf"][ib * 128:(ib + 1) * 128,
                                    dc * 128:(dc + 1) * 128],
                      sem=LD, transpose=True)
            ld_here = LD.n
            c.wait("tensor", LD, ld_here)
            if ib >= 1:
                c.wait("tensor", VE)
            for dc in range(16):
                c.mm(PF0[:, :], WA[:, base + dc * 128:base + (dc + 1) * 128],
                     XT[:, dc * 512:(dc + 1) * 512],
                     start=(dc == 0), stop=(dc == 15))
            for dc in range(16):
                c.mm(PF1[:, :], WA[:, base + 2048 + dc * 128:base + 2048 + (dc + 1) * 128],
                     XT[:, dc * 512:(dc + 1) * 512],
                     start=(dc == 0), stop=(dc == 15),
                     sem=(MM if dc == 15 else None))
            c.lu[("moesh", ib % 2)] = MM.n
            c.wait("scalar", MM)
            c.op("scalar", "activation", SGB[:, :], PF0[:, :], AF.Silu, sem=SC)
            c.wait("vector", SC)
            c.wait("vector", MM)
            c.op("vector", "tensor_tensor", hE[:, ib * 512:(ib + 1) * 512],
                 SGB[:, :], PF1[:, :], op=OP.mult, sem=VE)
        for sl in range(4):
            c.lu[("moew1", sl)] = MM.n
        for ds in range(4):
            for ic in range(11):
                slot = (ds * 11 + ic) % 8
                wslot = WB[:, slot * 512:slot * 512 + 512]
                c.wait("sync", MM, c.lu.get(("moew2", slot)))
                c.dma("sync", wslot,
                      c.t["sw3_bf"][ds * 512:(ds + 1) * 512,
                                    ic * 128:(ic + 1) * 128],
                      sem=LD, transpose=True)
                ld_here = LD.n
                c.wait("tensor", LD, ld_here)
                if ic == 0:
                    c.wait("tensor", VE)
                for st in range(4):
                    c.mm(PSB[:, st * 512:(st + 1) * 512],
                         hE[:, ic * 512 + st * 128:ic * 512 + st * 128 + 128],
                         wslot, start=(ic == 0), stop=(ic == 10),
                         sem=(MM if st == 3 else None))
                c.lu[("moew2", slot)] = MM.n
            c.wait("vector", MM)
            for st in range(4):
                acc = accf[:, st * 2048 + ds * 512:st * 2048 + ds * 512 + 512]
                c.op("vector", "scalar_tensor_tensor", acc,
                     PSB[:, st * 512:(st + 1) * 512], 1.0, acc,
                     op0=OP.mult, op1=OP.add, sem=(VE if st == 3 else None))
        for st in range(4):
            for ds in range(4):
                tok0 = sp * 512 + st * 128
                acc = accf[:, st * 2048 + ds * 512:st * 2048 + ds * 512 + 512]
                c.wait("scalar", VE)
                c.wait("scalar", ST)
                c.op("scalar", "activation", EA[:, 0:512], acc, AF.Silu, sem=SC)
                c.wait("gpsimd", VE)
                c.dma("gpsimd", EA[:, 512:1024],
                      c.t["s1"][tok0:tok0 + 128, ds * 512:(ds + 1) * 512],
                      sem=GLD)
                c.dma("gpsimd", EA[:, 1024:1536],
                      c.t["x1"][tok0:tok0 + 128, ds * 512:(ds + 1) * 512],
                      sem=GLD)
                c.wait("vector", SC)
                c.wait("vector", GLD)
                c.wait("vector", ST)
                c.op("vector", "tensor_tensor", EA[:, 1536:2048], EA[:, 0:512],
                     EA[:, 512:1024], op=OP.mult)
                c.op("vector", "tensor_tensor", EA[:, 1024:1536],
                     EA[:, 1536:2048], EA[:, 1024:1536], op=OP.add, sem=VE)
                c.wait("sync", VE)
                c.dma("sync", c.t["out"][tok0:tok0 + 128,
                                         ds * 512:(ds + 1) * 512],
                      EA[:, 1024:1536], sem=ST)
    c.wait("sync", sm["ST"])
    c.wait("gpsimd", sm["ST"])


PHASE_FNS = {
    "conv": ph_conv,
    "quant": ph_quant,
    "quantgu": lambda c, sm: _quant_one(c, sm, c.t["wgu"], c.t["wgu_hat"],
                                        2 * HF, D),
    "ln0": lambda c, sm: ph_ln(c, sm, "x", "xn_tok"),
    "l0qkv": ph_l0qkv,
    "l0attn": ph_l0attn,
    "l0wo": lambda c, sm: ph_wo(c, sm, "oT0", "wo_bf", "bo", "s0"),
    "l0ffn": ph_l0ffn,
    "ln1": lambda c, sm: ph_ln(c, sm, "x1", "xn1_tok"),
    "l1qkv": ph_l1qkv,
    "l1attn": ph_l1attn,
    "l1wo": lambda c, sm: ph_wo(c, sm, "oT1", "wo1_bf", None, "s1"),
    "l1moe": ph_l1moe,
}


def build_nc(phases=PHASES_ALL, feed=(), want=()):
    nc = bass.Bass("TRN2", target_bir_lowering=False)
    c = Ctx(nc)
    declare_tensors(c, phases, set(feed), set(want))
    sm = {k: c.sem(k) for k in ("LD", "GLD", "MM", "VE", "SC", "ST")}
    sm["GLD2"] = c.sem("GLD2", fenced=False)
    with contextlib.ExitStack() as stk:
        alloc_onchip(c, stk)
        CONSTF = c.sb["CONSTF"]
        cvals = (1e-5, SC0, SC1, -1.0, -0.5)
        for i, val in enumerate(cvals):
            c.op("vector", "memset", CONSTF[:, i:i + 1], val,
                 sem=(sm["VE"] if i == len(cvals) - 1 else None))
            nc.const_aps.register(F32, val, CONSTF[:, i:i + 1])
        c.fence()
        for ph in phases:
            PHASE_FNS[ph](c, sm)
            if ph == "quant":
                ph_conv_rest(c, sm)
            c.fence()
        c.wait("sync", sm["ST"])
        for s in c.sems + c.nofence:
            s.h = stk.enter_context(nc.semaphore(s.name))
        blk = stk.enter_context(nc.Block())
        for eng in ENGS:
            ops = c.ops[eng]
            if not ops:
                continue
            getattr(blk, eng)(lambda e, ops=ops: [f(e) for f in ops])
    return nc


def kernel(**inputs):
    shared = prep_inputs(inputs)
    x = np.ascontiguousarray(inputs["L0_attn_hidden_states"], dtype=np.float32)
    nc = build_nc()
    in_maps = []
    for core in range(8):
        m = dict(shared)
        m["x"] = np.ascontiguousarray(x[core])
        in_maps.append(m)
    res = run_bass_kernel_spmd(nc, in_maps, core_ids=list(range(8)))
    return np.stack([r["out"] for r in res.results], axis=0)
